# version 22
# baseline (speedup 1.0000x reference)
import numpy as np
import ml_dtypes
import concourse.bass as bass
import concourse.mybir as mybir
from concourse.bass_utils import run_bass_kernel_spmd

F32 = mybir.dt.float32
BF16 = mybir.dt.bfloat16
AF = mybir.ActivationFunctionType
ALU = mybir.AluOpType
AX = mybir.AxisListType

NCORES = 8
NB = 2
T = 2048
D = 1024
NT = 16
DT = 8
NE = 8
H = 2048
HT = 16

ENGS = ("sync", "tensor", "vector", "scalar", "gpsimd")

_DBG = 0
_STOP = None


class Sem:
    __slots__ = ("h", "n", "key")

    def __init__(self, h, key):
        self.h = h
        self.n = 0
        self.key = key


class Mst:
    __slots__ = ("sem", "val", "eng", "idx")

    def __init__(self, sem, val, eng, idx):
        self.sem = sem
        self.val = val
        self.eng = eng
        self.idx = idx


class SemPool:

    def __init__(self, sems):
        self.sems = sems
        self.rr = 0

    def next(self):
        s = self.sems[self.rr % len(self.sems)]
        self.rr += 1
        return s


def _flat(deps):
    out = []
    for d in deps:
        if d is None:
            continue
        if isinstance(d, (list, tuple)):
            out.extend(_flat(d))
        else:
            out.append(d)
    return out


class EM:

    def __init__(self, nc):
        self.nc = nc
        self.q = {e: [] for e in ENGS}
        self.waited = {e: {} for e in ENGS}
        self.idx = 0
        self.floor = 0
        self.bar = self.sem("bar")
        self.last_dma = {}
        self.sync_dmas = []
        self.scalar_dmas = []

    def sem(self, name):
        return Sem(self.nc.alloc_semaphore(name), name)

    def sempool(self, name, n):
        return SemPool([self.sem(f"{name}{i}") for i in range(n)])

    def emit(self, eng, build, deps=(), inc=None, dma=False, sdeps=()):
        self.idx += 1
        amt = 16 if dma else 1
        ms = None
        if isinstance(inc, SemPool):
            inc = inc.next()
        if inc is not None:
            inc.n += amt
            ms = Mst(inc, inc.n, None if dma else eng, self.idx)
            if dma:
                self.last_dma[inc.key] = ms
        wl = {}
        for d, strict in [(x, False) for x in _flat(deps)] + [(x, True) for x in _flat(sdeps)]:
            if d.idx <= self.floor:
                continue
            if d.eng == eng and not strict:
                continue
            prev = self.waited[eng].get(d.sem.key, 0)
            if d.val <= prev:
                continue
            cur = wl.get(d.sem.key)
            if cur is None or d.val > cur[1]:
                wl[d.sem.key] = (d.sem.h, d.val)
        for k, (h, v) in wl.items():
            self.waited[eng][k] = max(self.waited[eng].get(k, 0), v)
        self.q[eng].append((list(wl.values()), build, inc.h if inc is not None else None, amt))
        return ms

    def dma_total(self, sem):
        if isinstance(sem, SemPool):
            return [self.last_dma.get(s.key) for s in sem.sems]
        return self.last_dma.get(sem.key)

    def barrier(self, resources=()):
        self.bar.n += len(ENGS)
        tgt = self.bar.n
        for e in ENGS:
            if e == "sync":
                deps = [self.dma_total(s) for s in self.sync_dmas]
            elif e == "scalar":
                deps = [self.dma_total(s) for s in self.scalar_dmas]
            elif e == "gpsimd":
                deps = [self.dma_total(s) for s in getattr(self, "gpsimd_dmas", [])]
            else:
                deps = []

            def build(eng, tgt=tgt, e=e):
                if e in ("vector", "scalar", "tensor"):
                    eng.drain()
                eng.sem_inc(self.bar.h, 1)
                return eng.wait_ge(self.bar.h, tgt)

            self.emit(e, build, deps=deps)
        self.floor = self.idx
        for r in resources:
            r.reset()

    def finish(self, blk):
        def runner(name):
            def go(eng):
                for waits, build, inc_h, amt in self.q[name]:
                    for h, v in waits:
                        eng.wait_ge(h, v)
                    ins = build(eng)
                    if inc_h is not None:
                        ins.then_inc(inc_h, amt)
            return go

        blk.sync(runner("sync"))
        blk.tensor(runner("tensor"))
        blk.vector(runner("vector"))
        blk.scalar(runner("scalar"))
        blk.gpsimd(runner("gpsimd"))


class Banks:

    def __init__(self, pall):
        self.pall = pall
        self.free = [[] for _ in range(8)]
        self.busy = [False] * 8
        self.rr = 0

    def reset(self):
        assert not any(self.busy), "psum bank busy across barrier"
        self.free = [[] for _ in range(8)]
        self.busy = [False] * 8
        self.rr = 0

    def f32(self, k):
        return self.pall[:, k * 512:(k + 1) * 512]

    def bf16(self, k):
        return self.pall[:, k * 512:(k + 1) * 512].bitcast(BF16)

    def acquire(self, n=1, which=None):
        if which is None:
            which = []
            for _ in range(n):
                got = None
                for off in range(8):
                    s = (self.rr + off) % 8
                    if not self.busy[s]:
                        got = s
                        self.rr = (s + 1) % 8
                        break
                assert got is not None, "no free psum bank"
                which.append(got)
        deps = []
        for b in which:
            assert not self.busy[b], f"psum bank {b} busy"
            self.busy[b] = True
            deps.extend(self.free[b])
        return which, deps

    def release(self, which, ms):
        ms = _flat([ms])
        for b in which:
            assert self.busy[b]
            self.busy[b] = False
            self.free[b] = list(ms)


def align32(x):
    return (x + 31) & ~31


def build_program():
    nc = bass.Bass("TRN2", target_bir_lowering=False)

    d_x = nc.dram_tensor("x", [NB, T, D], F32, kind="ExternalInput").ap()
    d_wq = nc.dram_tensor("wq", [D, D], F32, kind="ExternalInput").ap()
    d_wk = nc.dram_tensor("wk", [D, D], F32, kind="ExternalInput").ap()
    d_wv = nc.dram_tensor("wv", [D, D], F32, kind="ExternalInput").ap()
    d_wo = nc.dram_tensor("wo", [D, D], F32, kind="ExternalInput").ap()
    d_rw1 = nc.dram_tensor("rw1", [D, D], F32, kind="ExternalInput").ap()
    d_ew1 = nc.dram_tensor("ew1", [NE, D, H], F32, kind="ExternalInput").ap()
    d_ew2 = nc.dram_tensor("ew2", [NE, H, D], F32, kind="ExternalInput").ap()
    d_idbf = nc.dram_tensor("idbf", [128, 128], BF16, kind="ExternalInput").ap()
    d_tri01 = nc.dram_tensor("tri01", [128, 128], BF16, kind="ExternalInput").ap()
    d_bias = nc.dram_tensor("biaspack", [128, 1280], F32, kind="ExternalInput").ap()
    d_rows = nc.dram_tensor("rowpack", [128, 1032], F32, kind="ExternalInput").ap()
    d_out = nc.dram_tensor("out", [NB, T, D], F32, kind="ExternalOutput").ap()
    if _DBG:
        d_dbg1 = nc.dram_tensor("dbg1", [128, NT, D], F32, kind="ExternalOutput").ap()
        d_dbg2 = nc.dram_tensor("dbg2", [128, 128], F32, kind="ExternalOutput").ap()
        d_dbgQ = nc.dram_tensor("dbgQ", [128, DT, T], BF16, kind="ExternalOutput").ap()
        d_dbgK = nc.dram_tensor("dbgK", [128, DT, T], BF16, kind="ExternalOutput").ap()
        d_dbgV = nc.dram_tensor("dbgV", [128, NT, D], BF16, kind="ExternalOutput").ap()
        d_dbgS = nc.dram_tensor("dbgS", [128, 68], F32, kind="ExternalOutput").ap()
        d_dbg3 = nc.dram_tensor("dbg3", [128, NT, D], F32, kind="ExternalOutput").ap()
        d_dbg4 = nc.dram_tensor("dbg4", [128, NT, D], F32, kind="ExternalOutput").ap()

    sc_w = {k: nc.dram_tensor(f"sc_{k}", [128, DT * D], BF16, kind="Internal").ap()
            for k in ("wk", "wv", "wq", "wo", "rw1")}
    sc_e1 = nc.dram_tensor("sc_e1", [NE, 128, DT * H], BF16, kind="Internal").ap()
    sc_e2 = nc.dram_tensor("sc_e2", [NE, 128, HT * D], BF16, kind="Internal").ap()

    cur = align32(nc.sbuf_base)

    def alloc(name, shape, dtype, at=None):
        nonlocal cur
        off = cur if at is None else at
        h = nc.alloc_sbuf_tensor_at(name, shape, dtype, offset=off)
        per = 1
        for s in shape[1:]:
            per *= s
        end = align32(off + per * mybir.dt.size(dtype))
        if at is None:
            cur = end
        else:
            cur = max(cur, end)
        return h

    ident_bf = alloc("ident_bf", [128, 128], BF16)
    tri01 = alloc("tri01", [128, 128], BF16)
    ones_bf = alloc("ones_bf", [1, 128], BF16)
    bqcol = alloc("bqcol", [128, 8], F32)
    bkcol = alloc("bkcol", [128, 8], F32)
    rb1col = alloc("rb1col", [128, 8], F32)
    eb1cols = alloc("eb1cols", [128, 128], F32)
    bv_bc = alloc("bv_bc", [128, D], BF16)
    rb2row = alloc("rb2row", [1, 8], BF16)
    ebplus = alloc("ebplus", [9, D], BF16)
    rw2sb = alloc("rw2sb", [128, 8, 8], BF16)
    w_sb = alloc("w_sb", [128, 128], F32)
    wbf_sb = alloc("wbf_sb", [128, 9], BF16)
    rsc = alloc("rsc", [128, 6, 8], F32)
    lnstats = alloc("lnstats", [128, 32], F32)
    stat = alloc("stat", [128, 68], F32)
    cv_f = alloc("cv_f", [128, 2, 1024], F32)
    out1 = alloc("out1", [128, NT, D], F32)

    r1 = cur
    BQT = alloc("BQT", [128, DT, T], BF16, at=r1)
    LN2T = alloc("LN2T", [128, DT, T], BF16, at=r1)
    r2 = cur
    KT = alloc("KT", [128, DT, T], BF16, at=r2)
    CH = alloc("CH", [128, DT, T], BF16, at=r2)
    r3 = cur
    V = alloc("V", [128, NT, D], BF16, at=r3)
    EW2 = alloc("EW2", [128, HT, D], BF16, at=r3)
    r4 = cur
    WS = alloc("WS", [128, DT, D], BF16, at=r4)
    EHT = alloc("EHT", [128, HT, 512], BF16, at=r4)
    wT_sb = alloc("wT_sb", [9, NT * 128], BF16)
    ln_bf = alloc("ln_bf", [128, 2, D], BF16)
    ppbuf = alloc("ppbuf", [128, T], BF16)
    ctxT_sb = alloc("ctxT_sb", [128, D], BF16)

    assert cur <= nc.sbuf_top, f"SBUF overflow: {cur} > {nc.sbuf_top}"

    pall = nc.alloc_psum_tensor("pall", [128, 4096], F32)

    em = EM(nc)
    tE = em.sem("tE")
    vE = em.sem("vE")
    sE = em.sem("sE")
    dmaW = em.sempool("dmaW", 4)
    dmaX = em.sempool("dmaX", 4)
    dmaG = em.sem("dmaG")
    em.sync_dmas = list(dmaW.sems)
    em.scalar_dmas = list(dmaX.sems)
    em.gpsimd_dmas = [dmaG]
    banks = Banks(pall[:])

    def S_(base, col, w=1):
        return stat[:, base + col: base + col + w]

    EPS_COL = 64

    def group(deps, psum_ap, ops, inc=tE):
        ms = None
        n = len(ops)
        for k, (l, r) in enumerate(ops):
            ms = em.emit(
                "tensor",
                lambda e, o=psum_ap, l=l, r=r, st=(k == 0), sp=(k == n - 1):
                    e.matmul(o, l, r, start=st, stop=sp),
                deps=deps if k == 0 else (),
                inc=inc if k == n - 1 else None)
        return ms

    stg_rr = [0]
    stg_free = [[], [], [], []]
    cvslots = [None, None, None, None]
    flush_prev = {"vector": None, "scalar": None}

    def stage_cast(chunks, war_deps, eng="vector", scratch=None, ring=2):
        ms = None
        pend_store = None
        for idx, (srcap, dstap) in enumerate(chunks):
            s = stg_rr[0] % ring
            stg_rr[0] += 1
            slot = cvslots[s]
            ld = em.emit("sync", lambda e, d=slot, sr=srcap: e.dma_start(out=d, in_=sr),
                         deps=stg_free[s], inc=dmaW, dma=True)
            if eng == "vector":
                ms = em.emit("vector",
                             lambda e, d=dstap, s2=slot: e.tensor_copy(out=d, in_=s2),
                             deps=[ld] + (war_deps if idx == 0 else []), inc=vE)
            else:
                ms = em.emit("scalar",
                             lambda e, d=dstap, s2=slot: e.activation(out=d, in_=s2,
                                                                      func=AF.Identity),
                             deps=[ld] + (war_deps if idx == 0 else []), inc=sE)
            stg_free[s] = [ms]
            if scratch is not None:
                if pend_store is not None:
                    sc_ap, sb_ap = pend_store
                    em.emit("gpsimd", lambda e, d=sc_ap, sr=sb_ap: e.dma_start(out=d, in_=sr),
                            deps=[ms], inc=dmaG, dma=True)
                pend_store = (scratch[idx], dstap)
        fl = em.emit(eng, (lambda e: e.memset(stat[:, 66:67], 0.0)) if eng == "vector"
                     else (lambda e: e.activation(out=stat[:, 67:68], in_=stat[:, 67:68],
                                                  func=AF.Identity)),
                     sdeps=[ms, flush_prev[eng]], inc=vE if eng == "vector" else sE)
        flush_prev[eng] = fl
        if scratch is not None and pend_store is not None:
            sc_ap, sb_ap = pend_store
            em.emit("gpsimd", lambda e, d=sc_ap, sr=sb_ap: e.dma_start(out=d, in_=sr),
                    deps=[fl], inc=dmaG, dma=True)
        return fl

    def stage_stepper(chunks, war_deps, eng="vector", scratch=None, ring=2):
        st = {"ms": None, "pend": None, "idx": 0}

        def step():
            idx = st["idx"]
            srcap, dstap = chunks[idx]
            s = stg_rr[0] % ring
            stg_rr[0] += 1
            slot = cvslots[s]
            ld = em.emit("sync", lambda e, d=slot, sr=srcap: e.dma_start(out=d, in_=sr),
                         deps=stg_free[s], inc=dmaW, dma=True)
            if eng == "vector":
                ms = em.emit("vector", lambda e, d=dstap, s2=slot: e.tensor_copy(out=d, in_=s2),
                             deps=[ld] + (war_deps if idx == 0 else []), inc=vE)
            else:
                ms = em.emit("scalar", lambda e, d=dstap, s2=slot: e.activation(
                    out=d, in_=s2, func=AF.Identity),
                             deps=[ld] + (war_deps if idx == 0 else []), inc=sE)
            stg_free[s] = [ms]
            if scratch is not None:
                if st["pend"] is not None:
                    sc_ap, sb_ap = st["pend"]
                    em.emit("gpsimd", lambda e, d=sc_ap, sr=sb_ap: e.dma_start(out=d, in_=sr),
                            deps=[ms], inc=dmaG, dma=True)
                st["pend"] = (scratch[idx], dstap)
            st["ms"] = ms
            st["idx"] += 1

        def fin():
            fl = em.emit(eng, (lambda e: e.memset(stat[:, 66:67], 0.0)) if eng == "vector"
                         else (lambda e: e.activation(out=stat[:, 67:68], in_=stat[:, 67:68],
                                                      func=AF.Identity)),
                         sdeps=[st["ms"], flush_prev[eng]], inc=vE if eng == "vector" else sE)
            flush_prev[eng] = fl
            if scratch is not None and st["pend"] is not None:
                sc_ap, sb_ap = st["pend"]
                em.emit("gpsimd", lambda e, d=sc_ap, sr=sb_ap: e.dma_start(out=d, in_=sr),
                        deps=[fl], inc=dmaG, dma=True)
            return fl
        return step, fin

    def wload(b, chunks, war_deps, sc_whole, dst_whole, eng="vector", sc_slices=None, ring=2):
        if b == 0:
            return stage_cast(chunks, war_deps, eng=eng, scratch=sc_slices, ring=ring)
        return em.emit("sync", lambda e, d=dst_whole, sr=sc_whole: e.dma_start(out=d, in_=sr),
                       deps=war_deps, inc=dmaW, dma=True)

    def sc_slices_w(sc, rows, width):
        out = []
        for r in range(rows):
            for h2 in range(width // 1024):
                out.append(sc[:, r * width + h2 * 1024: r * width + (h2 + 1) * 1024])
        return out

    def w_chunks(dsrc, dst, rows, width):
        out = []
        for r in range(rows):
            for h2 in range(width // 1024):
                out.append((dsrc[r * 128:(r + 1) * 128, h2 * 1024:(h2 + 1) * 1024],
                            dst[:, r, h2 * 1024:(h2 + 1) * 1024]))
        return out

    cv_flat = cv_f[:].rearrange("p a b -> p (a b)")
    bstage = cv_flat[:, 0:1280]
    rstage = cv_flat[:, 0:1032]
    l1 = em.emit("scalar", lambda e: e.dma_start(out=bstage, in_=d_bias), inc=dmaX, dma=True)
    em.emit("scalar", lambda e: e.dma_start(out=ident_bf[:], in_=d_idbf), inc=dmaX, dma=True)
    em.emit("scalar", lambda e: e.dma_start(out=tri01[:], in_=d_tri01), inc=dmaX, dma=True)
    tot_x = em.dma_total(dmaX)

    em.emit("vector", lambda e: e.memset(ones_bf[:], 1.0), inc=vE)
    em.emit("vector", lambda e: e.memset(wbf_sb[:, 8:9], 1.0), inc=vE)
    eps_ms = em.emit("vector", lambda e: e.memset(stat[:, EPS_COL:EPS_COL + 1], 1e-5), inc=vE)
    ones_ms = em.emit("vector", lambda e: e.memset(stat[:, EPS_COL + 1:EPS_COL + 2], 0.0), inc=vE)
    flush_prev["scalar"] = em.emit("scalar",
                                   lambda e: e.activation(out=stat[:, 67:68],
                                                          in_=stat[:, EPS_COL:EPS_COL + 1],
                                                          func=AF.Identity),
                                   deps=[eps_ms], inc=sE)

    bias_dep = [ones_ms]
    for fn in (
        lambda e: e.tensor_copy(out=bv_bc[:], in_=bstage[:, 0:1024]),
        lambda e: e.tensor_copy(out=bqcol[:], in_=bstage[:, 1024:1032]),
        lambda e: e.tensor_copy(out=bkcol[:], in_=bstage[:, 1032:1040]),
        lambda e: e.tensor_copy(out=rb1col[:], in_=bstage[:, 1040:1048]),
        lambda e: e.tensor_copy(out=eb1cols[:], in_=bstage[:, 1048:1176]),
        lambda e: e.tensor_copy(out=rw2sb[:].rearrange("p a b -> p (a b)"), in_=bstage[:, 1176:1240]),
    ):
        bias_dep.append(em.emit("vector", fn, deps=[l1, tot_x], inc=vE))
    rw2c = bias_dep[-1]
    l2 = em.emit("scalar", lambda e: e.dma_start(out=rstage, in_=d_rows),
                 deps=list(bias_dep), inc=dmaX, dma=True)
    tot_x2 = em.dma_total(dmaX)
    for fn in (
        lambda e: e.tensor_copy(out=ebplus[:], in_=rstage[0:9, 0:1024]),
        lambda e: e.tensor_copy(out=rb2row[:], in_=rstage[0:1, 1024:1032]),
    ):
        bias_dep.append(em.emit("vector", fn, deps=[tot_x2], inc=vE))
    P0DONE = list(bias_dep) + [tot_x2]
    cvslots[0] = cv_f[:, 0, :]
    cvslots[1] = cv_f[:, 1, :]
    cvslots[2] = ppbuf[:].bitcast(F32)
    cvslots[3] = ln_bf[:].rearrange("p a b -> p (a b)").bitcast(F32)
    stg_free[0] = list(P0DONE)
    stg_free[1] = list(P0DONE)

    ws_free = []
    st_done = {}
    ch_free = []
    ew2_free = []
    eht_war = []
    out_dma_ms = []

    def ln_stats(t, src_ap, src_deps, ln_prev, rstd_ap, nmrs_ap, sq_scratch, sq_war=(),
                 src_sdeps=()):
        base = (t % 4) * 16
        s1 = em.emit("vector", lambda e, base=base: e.reduce_sum(out=S_(base, 0), in_=src_ap, axis=AX.X),
                     deps=src_deps, sdeps=src_sdeps, inc=vE)
        sq = em.emit("vector",
                     lambda e, base=base: e.scalar_tensor_tensor(
                         out=sq_scratch, in0=src_ap, scalar=1.0,
                         in1=src_ap, op0=ALU.mult, op1=ALU.mult,
                         accum_out=S_(base, 1)),
                     deps=list(src_deps) + list(sq_war), sdeps=src_sdeps, inc=vE)
        msq = em.emit("vector", lambda e, base=base: e.scalar_tensor_tensor(
            out=S_(base, 3), in0=S_(base, 0), scalar=1.0 / (D * D), in1=S_(base, 0),
            op0=ALU.mult, op1=ALU.mult), sdeps=[s1], inc=vE)
        var = em.emit("vector", lambda e, base=base: e.tensor_scalar(out=S_(base, 4), in0=S_(base, 1),
                                                                    scalar1=1.0 / D, scalar2=S_(base, 3),
                                                                    op0=ALU.mult, op1=ALU.subtract),
                      deps=[ln_prev[t % 4]], sdeps=[sq, msq], inc=vE)
        lnv = em.emit("scalar", lambda e, base=base: e.activation(out=S_(base, 5), in_=S_(base, 4),
                                                                  func=AF.Ln,
                                                                  bias=stat[:, EPS_COL:EPS_COL + 1]),
                      deps=[var], inc=sE)
        ln_prev[t % 4] = lnv
        rstd = em.emit("scalar", lambda e, d=rstd_ap, base=base: e.activation(out=d, in_=S_(base, 5),
                                                                              func=AF.Exp,
                                                                              scale=-0.5), sdeps=[lnv], inc=sE)
        nmrs = em.emit("vector", lambda e, d=nmrs_ap, r=rstd_ap, base=base: e.scalar_tensor_tensor(
            out=d, in0=S_(base, 0), scalar=-1.0 / D, in1=r,
            op0=ALU.mult, op1=ALU.mult),
                       deps=[rstd], sdeps=[s1], inc=vE)
        return nmrs, rstd

    def ln_stage1(t, src_ap, src_deps, sq_war):
        base = (t % 4) * 16
        lb = ln_bf[:, t % 2, :]
        s1 = em.emit("vector", lambda e, base=base, src_ap=src_ap: e.reduce_sum(
            out=S_(base, 0), in_=src_ap, axis=AX.X), deps=src_deps, inc=vE)
        sq = em.emit("vector",
                     lambda e, base=base, src_ap=src_ap, lb=lb: e.scalar_tensor_tensor(
                         out=lb, in0=src_ap, scalar=1.0, in1=src_ap,
                         op0=ALU.mult, op1=ALU.mult, accum_out=S_(base, 1)),
                     deps=list(src_deps) + list(sq_war), inc=vE)
        return s1, sq

    def ln_stage2(t, s1, sq, ln_prev):
        base = (t % 4) * 16
        msq = em.emit("vector", lambda e, base=base: e.scalar_tensor_tensor(
            out=S_(base, 3), in0=S_(base, 0), scalar=1.0 / (D * D), in1=S_(base, 0),
            op0=ALU.mult, op1=ALU.mult), sdeps=[s1], inc=vE)
        var = em.emit("vector", lambda e, base=base: e.tensor_scalar(out=S_(base, 4), in0=S_(base, 1),
                                                                    scalar1=1.0 / D, scalar2=S_(base, 3),
                                                                    op0=ALU.mult, op1=ALU.subtract),
                      deps=[ln_prev[t % 4]], sdeps=[sq, msq], inc=vE)
        lnv = em.emit("scalar", lambda e, base=base: e.activation(out=S_(base, 5), in_=S_(base, 4),
                                                                  func=AF.Ln,
                                                                  bias=stat[:, EPS_COL:EPS_COL + 1]),
                      deps=[var], inc=sE)
        ln_prev[t % 4] = lnv
        rstd = em.emit("scalar", lambda e, base=base: e.activation(out=S_(base, 6), in_=S_(base, 5),
                                                                   func=AF.Exp,
                                                                   scale=-0.5), sdeps=[lnv], inc=sE)
        nmrs = em.emit("vector", lambda e, base=base: e.scalar_tensor_tensor(
            out=S_(base, 7), in0=S_(base, 0), scalar=-1.0 / D, in1=S_(base, 6),
            op0=ALU.mult, op1=ALU.mult),
                       deps=[rstd], sdeps=[s1], inc=vE)
        return nmrs, rstd

    def ln_finish(t, src_ap, rstd_ap, nmrs_ap, stats_ms, dst_dep):
        lb = ln_bf[:, t % 2, :]
        norm = em.emit("scalar",
                       lambda e, lb=lb, r=rstd_ap, nm=nmrs_ap: e.activation(
                           out=lb, in_=src_ap, func=AF.Identity,
                           bias=nm, scale=r),
                       deps=[stats_ms[0]] + dst_dep["lnbf_war"][t % 2],
                       sdeps=[stats_ms[1]], inc=sE)
        bnk, bd = banks.acquire(1)
        trs = None
        for k in range(8):
            trs = em.emit("tensor",
                          lambda e, k=k, b=bnk[0], lb=lb: e.transpose(
                              banks.bf16(b)[:, k * 128:(k + 1) * 128],
                              lb[:, k * 128:(k + 1) * 128], ident_bf[:]),
                          deps=([norm] + bd + dst_dep.get("tr_dep", [])) if k == 0 else (),
                          inc=tE if k == 7 else None)
        dst = dst_dep["dstT"]
        ev = em.emit("vector",
                     lambda e, b=bnk[0]: e.tensor_copy(
                         out=dst[:, :, t * 128:(t + 1) * 128],
                         in_=banks.bf16(b).rearrange("p (a q) -> p a q", a=8)),
                     deps=[trs] + dst_dep["dst_war"], inc=vE)
        banks.release(bnk, [ev])
        return ev, norm, trs

    def ln_tile(tag, t, src_ap, dst_dep, b):
        base = (t % 4) * 16
        lb = ln_bf[:, t % 2, :]
        stats = ln_stats(t, src_ap, dst_dep["src_ready"], dst_dep["ln_prev"],
                         S_(base, 6), S_(base, 7), lb, sq_war=dst_dep["lnbf_war"][t % 2])
        return ln_finish(t, src_ap, S_(base, 6), S_(base, 7), stats, dst_dep)

    def emit_batch(b):
        nonlocal ws_free, ch_free, ew2_free, eht_war, out_dma_ms
        lnT_ready = [None] * NT
        lnbf_war = [[], []]
        ln_prev = [None, None, None, None]

        wk_step = wk_fin = None
        if b == 0:
            wk_step, wk_fin = stage_stepper(w_chunks(d_wk, WS, DT, D),
                                            ws_free + [st_done.get("WS")],
                                            scratch=sc_slices_w(sc_w["wk"], DT, D))
            wk_ld = None
        else:
            wk_ld = em.emit("sync", lambda e: e.dma_start(
                out=WS[:].rearrange("p a b -> p (a b)"), in_=sc_w["wk"]),
                deps=ws_free + [st_done.get("WS")], inc=dmaW, dma=True)

        kT_ready = [None] * 4
        last_k_group = None

        def emit_kT_chunk(c):
            nonlocal last_k_group
            cdep = [lnT_ready[4 * c + i] for i in range(4)]
            for a in range(8):
                bnk, bd = banks.acquire(1)
                g = group(cdep + bd + [wk_ld],
                          banks.f32(bnk[0]),
                          [(WS[:, d, a * 128:(a + 1) * 128], BQT[:, d, c * 512:(c + 1) * 512])
                           for d in range(DT)])
                ev = em.emit("scalar",
                             lambda e, a=a, c=c, b_=bnk[0]: e.activation(
                                 out=KT[:, a, c * 512:(c + 1) * 512], in_=banks.f32(b_),
                                 func=AF.Identity, bias=bkcol[:, a:a + 1]),
                             deps=[g], inc=sE)
                banks.release(bnk, [ev])
                last_k_group = g
                kT_ready[c] = ev

        def finish_tile(tp, s1p, sqp):
            stats = ln_stage2(tp, s1p, sqp, ln_prev)
            base_p = (tp % 4) * 16
            dd = {"lnbf_war": lnbf_war, "dstT": BQT, "dst_war": [],
                  "tr_dep": (P0DONE if (b == 0 and tp == 0) else [])}
            ev, norm, trs = ln_finish(tp, out1[:, tp, :], S_(base_p, 6), S_(base_p, 7),
                                      stats, dd)
            lnT_ready[tp] = ev
            lnbf_war[tp % 2] = [trs]

        x_ld = [None] * NT
        pend = None
        ksched = {7: 0, 11: 1, 15: 2} if b == 0 else {4: 0, 8: 1, 12: 2}
        for t in range(NT):
            p0 = P0DONE if (b == 0 and t == 0) else []
            x_ld[t] = em.emit("scalar",
                              lambda e, t=t: e.dma_start(out=out1[:, t, :],
                                                         in_=d_x[b, t * 128:(t + 1) * 128, :]),
                              deps=p0 + ([x_ld[t - 4]] if t >= 4 else []), inc=dmaX, dma=True)
            if wk_step is not None and t < 8:
                wk_step()
                if t == 7:
                    wk_ld = wk_fin()
            s1, sq = ln_stage1(t, out1[:, t, :], p0 + [x_ld[t]], lnbf_war[t % 2])
            if pend is not None:
                finish_tile(*pend)
            pend = (t, s1, sq)
            c = ksched.get(t)
            if c is not None:
                emit_kT_chunk(c)
        finish_tile(*pend)
        emit_kT_chunk(3)


        def dbg_flush(deps):
            f1 = em.emit("vector", lambda e: e.memset(stat[:, 67:68], 0.0), deps=deps,
                         sdeps=deps, inc=vE)
            return em.emit("scalar", lambda e: e.activation(out=stat[:, 67:68], in_=stat[:, 67:68],
                                                            func=AF.Identity),
                           deps=[f1], inc=sE)

        if _DBG and _STOP == "ln" and b == 0:
            fl = dbg_flush(lnT_ready)
            em.emit("scalar", lambda e: e.dma_start(out=d_dbgQ, in_=BQT[:]),
                    deps=[fl], inc=dmaX, dma=True)
            em.emit("scalar", lambda e: e.dma_start(out=d_dbgS, in_=stat[:]),
                    deps=[fl], inc=dmaX, dma=True)
            em.emit("scalar", lambda e: e.dma_start(out=d_dbg1, in_=out1[:]),
                    deps=[fl], inc=dmaX, dma=True)
            return "stop"
        st_done["WS"] = em.dma_total(dmaG)
        wv_ld = wload(b, w_chunks(d_wv, WS, DT, D), [last_k_group, st_done.get("WS")], sc_w["wv"],
                      WS[:].rearrange("p a b -> p (a b)"), sc_slices=sc_slices_w(sc_w["wv"], DT, D))
        v_ready = [None] * NT
        last_v_group = None
        for c in range(4):
            for t in range(4 * c, 4 * c + 4):
                for n in range(2):
                    bnk, bd = banks.acquire(1)
                    g = group([lnT_ready[t], wv_ld] + bd,
                              banks.f32(bnk[0]),
                              [(BQT[:, d, t * 128:(t + 1) * 128], WS[:, d, n * 512:(n + 1) * 512])
                               for d in range(DT)])
                    ev = em.emit("vector",
                                 lambda e, t=t, n=n, b_=bnk[0]: e.tensor_tensor(
                                     out=V[:, t, n * 512:(n + 1) * 512], in0=banks.f32(b_),
                                     in1=bv_bc[:, n * 512:(n + 1) * 512], op=ALU.add),
                                 deps=[g], inc=vE)
                    banks.release(bnk, [ev])
                    last_v_group = g
                    v_ready[t] = ev
        st_done["WS"] = em.dma_total(dmaG)
        wq_ld = wload(b, w_chunks(d_wq, WS, DT, D), [last_v_group, st_done.get("WS")], sc_w["wq"],
                      WS[:].rearrange("p a b -> p (a b)"), sc_slices=sc_slices_w(sc_w["wq"], DT, D))
        qT_ready = [None] * 4
        last_q_group = None
        for c in range(4):
            bl, bd = banks.acquire(8)
            gms = []
            for a in range(8):
                g = group((bd + [wq_ld]) if a == 0 else [],
                          banks.f32(bl[a]),
                          [(WS[:, d, a * 128:(a + 1) * 128], BQT[:, d, c * 512:(c + 1) * 512])
                           for d in range(DT)])
                gms.append(g)
            evs = []
            for a in range(8):
                ev = em.emit("scalar",
                             lambda e, a=a, c=c, b_=bl[a]: e.activation(
                                 out=BQT[:, a, c * 512:(c + 1) * 512], in_=banks.f32(b_),
                                 func=AF.Identity, bias=bqcol[:, a:a + 1]),
                             deps=[gms[-1]], inc=sE)
                evs.append(ev)
                banks.release([bl[a]], [ev])
            last_q_group = gms[-1]
            qT_ready[c] = evs[-1]

        if _DBG and _STOP == "p1" and b == 0:
            fl = dbg_flush(qT_ready + kT_ready + v_ready)
            em.emit("scalar", lambda e: e.dma_start(out=d_dbgQ, in_=BQT[:]),
                    deps=[fl], inc=dmaX, dma=True)
            em.emit("scalar", lambda e: e.dma_start(out=d_dbgK, in_=KT[:]),
                    deps=[fl], inc=dmaX, dma=True)
            em.emit("scalar", lambda e: e.dma_start(out=d_dbgV, in_=V[:]),
                    deps=[fl], inc=dmaX, dma=True)
            em.emit("scalar", lambda e: e.dma_start(out=d_dbg1, in_=out1[:]),
                    deps=[fl], inc=dmaX, dma=True)
            em.emit("scalar", lambda e: e.dma_start(out=d_dbgS, in_=stat[:]),
                    deps=[fl], inc=dmaX, dma=True)
            return "stop"
        st_done["WS"] = em.dma_total(dmaG)
        wo_ld = wload(b, w_chunks(d_wo, WS, DT, D), [last_q_group, st_done.get("WS")], sc_w["wo"],
                      WS[:].rearrange("p a b -> p (a b)"), sc_slices=sc_slices_w(sc_w["wo"], DT, D))

        pend_attn = None
        attn_done = [None] * NT
        ctx_read_ppbuf = []
        ctx_sb_readers = []
        ctx_evac = []
        prev_exp_par = [None] * 4
        rs_ms = [None] * NT
        ln2_done = [None] * NT
        ln2_sq_last = []
        prev_ppbuf_rd = []

        def pre_ln2(t):
            stats = ln_stats(t, out1[:, t, :], [attn_done[t]], ln_prev,
                             lnstats[:, 2 * t:2 * t + 1], lnstats[:, 2 * t + 1:2 * t + 2],
                             cv_f[:, t % 2, :], src_sdeps=[attn_done[t]])
            ln2_done[t] = stats
            ln2_sq_last.append(stats[0])
            del ln2_sq_last[:-1]

        def emit_attn_groups(i):
            base = (i % 4) * 16
            bw, bdep = banks.acquire(2, which=[4, 5])
            gs = []
            for n in range(2):
                g = group((ctx_evac + [wo_ld] + bdep) if n == 0 else [],
                          banks.f32(4 + n),
                          [(ctxT_sb[:, a * 128:(a + 1) * 128], WS[:, a, n * 512:(n + 1) * 512])
                           for a in range(8)])
                gs.append(g)

            def emit_stts():
                sts = []
                for n in range(2):
                    st = em.emit("vector",
                                 lambda e, i=i, n=n, base=base: e.scalar_tensor_tensor(
                                     out=out1[:, i, n * 512:(n + 1) * 512],
                                     in0=banks.f32(4 + n), scalar=S_(base, 15),
                                     in1=out1[:, i, n * 512:(n + 1) * 512],
                                     op0=ALU.mult, op1=ALU.add),
                                 deps=[gs[n]], sdeps=[rs_ms[i]], inc=vE)
                    sts.append(st)
                banks.release([4, 5], sts)
                attn_done[i] = sts
            return gs, emit_stts

        s_last_ms = None

        def emit_scores(i):
            nonlocal s_last_ms
            sg = []
            for j in range(i // 4 + 1):
                wj = min(512, (i + 1) * 128 - j * 512)
                bw, bdep = banks.acquire(1, which=[j])
                g = group(bdep + [qT_ready[i // 4], kT_ready[j]],
                          pall[:, j * 512: j * 512 + wj],
                          [(BQT[:, a, i * 128:(i + 1) * 128], KT[:, a, j * 512:j * 512 + wj])
                           for a in range(8)])
                sg.append(g)
            s_last_ms = sg[-1]
            return sg

        next_scores = emit_scores(0)
        for i in range(NT):
            base = (i % 4) * 16
            nch = i // 4 + 1
            s_groups = next_scores
            attn_stt = None
            if pend_attn is not None:
                ctx_sb_readers, attn_stt = pend_attn()
                pend_attn = None
            mx = []
            for j in range(nch):
                wj = min(512, (i + 1) * 128 - j * 512)
                deps = [s_groups[j]]
                mx.append(em.emit("vector",
                                  lambda e, j=j, wj=wj, base=base: e.reduce_max(
                                      out=S_(base, 8 + j), in_=pall[:, j * 512: j * 512 + wj], axis=AX.X),
                                  deps=deps, inc=vE))
            nmneg = em.emit("vector", lambda e, nch=nch, base=base: e.reduce_max(out=S_(base, 12),
                                                                                in_=S_(base, 8, nch),
                                                                                axis=AX.X, negate=True),
                            deps=[prev_exp_par[i % 4]], sdeps=mx, inc=vE)
            nm32 = em.emit("vector", lambda e, base=base: e.tensor_scalar(out=S_(base, 13), in0=S_(base, 12),
                                                                          scalar1=1.0 / 32, scalar2=None,
                                                                          op0=ALU.mult), sdeps=[nmneg], inc=vE)
            if attn_stt is not None:
                attn_stt()
            exp_ms = []
            pgates = []
            for j in range(nch):
                wj = min(512, (i + 1) * 128 - j * 512)
                ex = em.emit("scalar",
                             lambda e, j=j, wj=wj, base=base: e.activation(
                                 out=ppbuf[:, j * 512: j * 512 + wj],
                                 in_=pall[:, j * 512: j * 512 + wj], func=AF.Exp,
                                 scale=1.0 / 32, bias=S_(base, 13)),
                             deps=[nm32] + ctx_read_ppbuf + prev_ppbuf_rd, inc=sE)
                exp_ms.append(ex)
                banks.release([j], [ex])
                if j == nch - 1:
                    zero_ms = em.emit("vector",
                                      lambda e, i=i: e.tensor_tensor(
                                          out=ppbuf[:, i * 128:(i + 1) * 128],
                                          in0=ppbuf[:, i * 128:(i + 1) * 128],
                                          in1=tri01[:], op=ALU.mult),
                                      deps=[ex], inc=vE)
                    pgates.append([zero_ms])
                else:
                    pgates.append([ex])
            next_scores = emit_scores(i + 1) if i + 1 < NT else None
            psum_ms = []
            pt_ev = []
            for j in range(nch):
                wj = min(512, (i + 1) * 128 - j * 512)
                ex = exp_ms[j]
                pgate = pgates[j]
                psum_ms.append(em.emit("vector",
                                       lambda e, j=j, wj=wj, base=base: e.reduce_sum(
                                           out=S_(base, 8 + j),
                                           in_=ppbuf[:, j * 512: j * 512 + wj], axis=AX.X),
                                       deps=[ex], sdeps=pgate, inc=vE))
                nkc = wj // 128
                bnk, bdep = banks.acquire(1, which=[6 + (j % 2)])
                trs = None
                for k in range(nkc):
                    trs = em.emit("tensor",
                                  lambda e, j=j, k=k, b_=bnk[0]: e.transpose(
                                      banks.bf16(b_)[:, k * 128:(k + 1) * 128],
                                      ppbuf[:, j * 512 + k * 128: j * 512 + (k + 1) * 128],
                                      ident_bf[:]),
                                  deps=(pgate + bdep) if k == 0 else (),
                                  inc=tE if k == nkc - 1 else None)
                ev = em.emit("vector",
                             lambda e, j=j, wj=wj, b_=bnk[0]: e.tensor_copy(
                                 out=ppbuf[:, j * 512: j * 512 + wj],
                                 in_=banks.bf16(b_)[:, 0:wj]),
                             deps=[trs], sdeps=[psum_ms[-1]], inc=vE)
                banks.release(bnk, [ev])
                pt_ev.append(ev)
            prev_exp_par[i % 4] = exp_ms[-1]
            prev_ppbuf_rd = list(psum_ms) + pt_ev
            rsum = em.emit("vector", lambda e, nch=nch, base=base: e.reduce_sum(out=S_(base, 14),
                                                                               in_=S_(base, 8, nch),
                                                                               axis=AX.X),
                           sdeps=psum_ms, inc=vE)
            rs_ms[i] = em.emit("vector", lambda e, base=base: e.reciprocal(out=S_(base, 15), in_=S_(base, 14)),
                               sdeps=[rsum], inc=vE)
            bw, bdep = banks.acquire(2, which=[4, 5])
            ctx_g = []
            for a in range(8):
                g = group((pt_ev + bdep + [v_ready[i]]) if a == 0 else [],
                          pall[:, 2048 + a * 128: 2048 + (a + 1) * 128],
                          [(V[:, kc, a * 128:(a + 1) * 128], ppbuf[:, kc * 128:(kc + 1) * 128])
                           for kc in range(i + 1)])
                ctx_g.append(g)
            ctx_read_ppbuf = [ctx_g[-1]]
            ctx_last_ms = ctx_g[-1]
            ev = em.emit("vector",
                         lambda e: e.tensor_copy(out=ctxT_sb[:], in_=pall[:, 2048:3072]),
                         deps=[ctx_g[-1]] + ctx_sb_readers, inc=vE)
            ctx_evac = [ev]
            banks.release([4, 5], [ev])
            pend_attn = (lambda i=i: emit_attn_groups(i))
            if i >= 1:
                pre_ln2(i - 1)
        if pend_attn is not None:
            gs, sttfn = pend_attn()
            sttfn()
            pend_attn = None
        pre_ln2(NT - 1)
        stg_free[0] = stg_free[0] + ln2_sq_last
        stg_free[1] = stg_free[1] + ln2_sq_last
        ws_free = [g for g in (attn_done[NT - 1] or [])]
        if _DBG and b == 0:
            fl = dbg_flush([m for ms in attn_done for m in (ms or [])] + [rs_ms[14], rs_ms[15]])
            em.emit("scalar", lambda e: e.dma_start(out=d_dbg1, in_=out1[:]),
                    deps=[fl], inc=dmaX, dma=True)
            em.emit("scalar", lambda e: e.dma_start(out=d_dbgS, in_=stat[:]),
                    deps=[fl], inc=dmaX, dma=True)
        if _STOP == "p2" and b == 0:
            return "stop"

        add1_par = [None, None]
        st_done["EW2"] = em.dma_total(dmaG)
        rw1_ld = wload(b, w_chunks(d_rw1, EW2, DT, D), [ctx_last_ms, st_done.get("EW2")], sc_w["rw1"],
                      EW2[:, 0:8, :].rearrange("p a b -> p (a b)"), sc_slices=sc_slices_w(sc_w["rw1"], DT, D))
        ln2_ready = [None] * NT
        hT_ready = [None] * 4
        w_ready = [None] * NT
        eb2_added = [None] * NT
        last_h_group = None
        last_logit_group = None
        wT_readers = []
        pend_tails = []

        def make_tail(t):
            def tail():
                nonlocal wT_readers
                wb = em.emit("vector", lambda e, t=t: e.tensor_copy(
                    out=wbf_sb[:, 0:8], in_=w_sb[:, t * 8:(t + 1) * 8]),
                             deps=wT_readers, sdeps=[w_ready[t]], inc=vE)
                bnk2, bd2 = banks.acquire(1)
                tr = em.emit("tensor",
                             lambda e, b_=bnk2[0]: e.transpose(banks.bf16(b_)[0:9, 0:128],
                                                               wbf_sb[:], ident_bf[:]),
                             deps=[wb] + bd2, inc=tE)
                wtc = em.emit("vector",
                              lambda e, t=t, b_=bnk2[0]: e.tensor_copy(
                                  out=wT_sb[:, t * 128:(t + 1) * 128],
                                  in_=banks.bf16(b_)[0:9, 0:128]),
                              deps=[tr], inc=vE)
                banks.release(bnk2, [wtc])
                bl2, bdd = banks.acquire(2)
                adds = []
                for n in range(2):
                    ops2 = [(wT_sb[:, t * 128:(t + 1) * 128], ebplus[:, n * 512:(n + 1) * 512])]
                    g2 = group([wtc] + (bdd if n == 0 else []), banks.f32(bl2[n]), ops2)
                    ad = em.emit("vector",
                                 lambda e, t=t, n=n, b_=bl2[n]: e.tensor_tensor(
                                     out=out1[:, t, n * 512:(n + 1) * 512],
                                     in0=out1[:, t, n * 512:(n + 1) * 512],
                                     in1=banks.f32(b_), op=ALU.add),
                                 deps=[g2], inc=vE)
                    adds.append(ad)
                    banks.release([bl2[n]], [ad])
                wT_readers = [tr]
                eb2_added[t] = adds
            return tail

        for c in range(4):
            for t in range(4 * c, 4 * c + 4):
                dd = {"lnbf_war": lnbf_war, "dstT": LN2T, "dst_war": [s_last_ms]}
                ev, norm, trs = ln_finish(t, out1[:, t, :],
                                          lnstats[:, 2 * t:2 * t + 1],
                                          lnstats[:, 2 * t + 1:2 * t + 2],
                                          ln2_done[t], dd)
                ln2_ready[t] = ev
                add1 = em.emit("vector",
                               lambda e, t=t: e.tensor_tensor(out=out1[:, t, :], in0=out1[:, t, :],
                                                              in1=ln_bf[:, t % 2, :], op=ALU.add),
                               deps=[norm], inc=vE)
                lnbf_war[t % 2] = [trs, add1]
                add1_par[t % 2] = add1
            cdep = [ln2_ready[4 * c + i] for i in range(4)]
            for a in range(8):
                bnk, bd = banks.acquire(1)
                g = group(cdep + bd + [rw1_ld],
                          banks.f32(bnk[0]),
                          [(EW2[:, d, a * 128:(a + 1) * 128], LN2T[:, d, c * 512:(c + 1) * 512])
                           for d in range(DT)])
                ev = em.emit("scalar",
                             lambda e, a=a, c=c, b_=bnk[0]: e.activation(
                                 out=CH[:, a, c * 512:(c + 1) * 512], in_=banks.f32(b_),
                                 func=AF.Silu, bias=rb1col[:, a:a + 1]),
                             deps=[g, s_last_ms], inc=sE)
                banks.release(bnk, [ev])
                last_h_group = g
                hT_ready[c] = ev
            for tl in pend_tails:
                tl()
            pend_tails = []
            for t in range(4 * c, 4 * c + 4):
                bnk, bd = banks.acquire(1)
                pl = banks.f32(bnk[0])[:, 0:8]
                ops = [(CH[:, a, t * 128:(t + 1) * 128], rw2sb[:, a, :]) for a in range(8)]
                ops.append((ones_bf[0:1, :], rb2row[0:1, :]))
                g = group([hT_ready[c], rw2c] + bd + bias_dep, pl, ops)
                last_logit_group = g
                m1 = em.emit("vector", lambda e, pl=pl: e.reduce_max(out=S_(0, 8), in_=pl, axis=AX.X),
                             deps=[g], inc=vE)
                ngm = em.emit("vector", lambda e: e.tensor_scalar(out=S_(0, 9), in0=S_(0, 8), scalar1=-1.0,
                                                                  scalar2=None, op0=ALU.mult),
                              sdeps=[m1], inc=vE)
                eq = em.emit("vector", lambda e, pl=pl: e.tensor_scalar(out=rsc[:, 0, :], in0=pl,
                                                                        scalar1=S_(0, 8), scalar2=None,
                                                                        op0=ALU.is_equal),
                             sdeps=[m1], inc=vE)
                l2 = em.emit("vector", lambda e, pl=pl: e.scalar_tensor_tensor(out=rsc[:, 1, :], in0=rsc[:, 0, :],
                                                                               scalar=-1e9, in1=pl,
                                                                               op0=ALU.mult, op1=ALU.add),
                             sdeps=[eq], inc=vE)
                m2 = em.emit("vector", lambda e: e.reduce_max(out=S_(0, 10), in_=rsc[:, 1, :], axis=AX.X),
                             sdeps=[l2], inc=vE)
                ge = em.emit("vector", lambda e, pl=pl: e.tensor_scalar(out=rsc[:, 2, :], in0=pl,
                                                                        scalar1=S_(0, 10), scalar2=None,
                                                                        op0=ALU.is_ge),
                             sdeps=[m2], inc=vE)
                exr = rsc[:, 4 + (t % 2), :]
                ex = em.emit("scalar",
                             lambda e, pl=pl, exr=exr: e.activation(out=exr, in_=pl, func=AF.Exp,
                                                                    bias=S_(0, 9)),
                             deps=[ngm] + ([] if t < 2 else [w_ready[t - 2]]), inc=sE)
                banks.release(bnk, [ex, ge])
                me = em.emit("vector", lambda e, exr=exr: e.tensor_tensor(out=rsc[:, 3, :], in0=exr,
                                                                          in1=rsc[:, 2, :], op=ALU.mult),
                             deps=[ex], sdeps=[ge], inc=vE)
                rsm = em.emit("vector", lambda e: e.reduce_sum(out=S_(0, 11), in_=rsc[:, 3, :], axis=AX.X),
                              sdeps=[me], inc=vE)
                rcp = em.emit("vector", lambda e: e.reciprocal(out=S_(0, 12), in_=S_(0, 11)),
                              sdeps=[rsm], inc=vE)
                wt = em.emit("vector", lambda e, t=t: e.tensor_scalar(out=w_sb[:, t * 8:(t + 1) * 8],
                                                                      in0=rsc[:, 3, :], scalar1=S_(0, 12),
                                                                      scalar2=None, op0=ALU.mult),
                             sdeps=[rcp], inc=vE)
                w_ready[t] = wt
                pend_tails.append(make_tail(t))
        ch_free = [last_logit_group]
        ew2_free = [last_h_group]
        if _DBG and b == 0:
            fl = dbg_flush([m for m in w_ready] + ln2_ready + [m for m in eb2_added if m] + [nm for nm, _ in ln2_done])
            em.emit("scalar", lambda e: e.dma_start(out=d_dbg2, in_=w_sb[:]),
                    deps=[fl], inc=dmaX, dma=True)
            em.emit("scalar", lambda e: e.dma_start(out=d_dbgK, in_=LN2T[:]),
                    deps=[fl], inc=dmaX, dma=True)
            em.emit("scalar", lambda e: e.dma_start(out=d_dbgS[:, 0:32], in_=lnstats[:]),
                    deps=[fl], inc=dmaX, dma=True)
            em.emit("scalar", lambda e: e.dma_start(out=d_dbg1, in_=out1[:]),
                    deps=[fl], inc=dmaX, dma=True)
        if _STOP == "p3" and b == 0:
            return "stop"

        store_q = []
        if b == 0:
            stg_free[2] = [ctx_last_ms]
            stg_free[3] = [ln2_ready[NT - 1]] + lnbf_war[0] + lnbf_war[1]
        for tl in pend_tails:
            tl()
        pend_tails = []
        ch_chunks = w_chunks(d_ew1[0], CH, DT, H)
        ch_sc = sc_slices_w(sc_e1[0], DT, H)
        if b == 0:
            h0 = stage_cast([ch_chunks[i] for i in range(0, 16, 2)],
                            ch_free + [st_done.get("CH")], eng="scalar",
                            scratch=[ch_sc[i] for i in range(0, 16, 2)], ring=4)
            h1 = stage_cast([ch_chunks[i] for i in range(1, 16, 2)], [], eng="scalar",
                            scratch=[ch_sc[i] for i in range(1, 16, 2)], ring=4)
            ew1_pair = (h0, h1)
        else:
            ms0 = em.emit("sync", lambda e: e.dma_start(
                out=CH[:].rearrange("p a b -> p (a b)"), in_=sc_e1[0]),
                          deps=ch_free + [st_done.get("CH")], inc=dmaW, dma=True)
            ew1_pair = (ms0, ms0)
        st_done["CH"] = em.dma_total(dmaG)
        st_done["EW2"] = em.dma_total(dmaG)
        ew2_ms = wload(b, w_chunks(d_ew2[0], EW2, HT, D), ew2_free + [st_done.get("EW2")], sc_e2[0],
                       EW2[:].rearrange("p a b -> p (a b)"),
                       sc_slices=sc_slices_w(sc_e2[0], HT, D), ring=4)
        for e_ in range(NE):
            last_eh_group = None
            last_eo_group = None
            for c in range(4):
                silu_last = None
                for m in range(HT):
                    bnk, bd = banks.acquire(1)
                    g = group(bd + [ew1_pair[0] if m < 8 else ew1_pair[1]],
                              banks.f32(bnk[0]),
                              [(CH[:, d, m * 128:(m + 1) * 128], LN2T[:, d, c * 512:(c + 1) * 512])
                               for d in range(DT)])
                    ev = em.emit("scalar",
                                 lambda e, ee=e_, m=m, b_=bnk[0]: e.activation(
                                     out=EHT[:, m, :], in_=banks.f32(b_), func=AF.Silu,
                                     bias=eb1cols[:, ee * 16 + m: ee * 16 + m + 1]),
                                 deps=[g] + eht_war, inc=sE)
                    banks.release(bnk, [ev])
                    silu_last = ev
                    last_eh_group = g
                silu_flush = em.emit("scalar",
                                     lambda e: e.activation(out=stat[:, 67:68], in_=stat[:, 67:68],
                                                            func=AF.Identity),
                                     sdeps=[silu_last, flush_prev["scalar"]], inc=sE)
                flush_prev["scalar"] = silu_flush
                if c == 3 and e_ < NE - 1:
                    st_done["CH"] = em.dma_total(dmaG)
                    next_ew1_ms = wload(b, w_chunks(d_ew1[e_ + 1], CH, DT, H),
                                        [last_eh_group, st_done.get("CH")], sc_e1[e_ + 1],
                                        CH[:].rearrange("p a b -> p (a b)"), eng="scalar",
                                        sc_slices=sc_slices_w(sc_e1[e_ + 1], DT, H), ring=4)
                eo_groups = []
                for s4 in range(4):
                    t = 4 * c + s4
                    bl, bd = banks.acquire(2)
                    evs = []
                    for n in range(2):
                        g = group(([silu_flush, ew2_ms] + bd) if n == 0 else [silu_flush],
                                  banks.f32(bl[n]),
                                  [(EHT[:, m, s4 * 128:(s4 + 1) * 128], EW2[:, m, n * 512:(n + 1) * 512])
                                   for m in range(HT)])
                        eo_groups.append(g)
                        st = em.emit("vector",
                                     lambda e, t=t, n=n, ee=e_, b_=bl[n]: e.scalar_tensor_tensor(
                                         out=out1[:, t, n * 512:(n + 1) * 512],
                                         in0=banks.f32(b_),
                                         scalar=w_sb[:, t * 8 + ee: t * 8 + ee + 1],
                                         in1=out1[:, t, n * 512:(n + 1) * 512],
                                         op0=ALU.mult, op1=ALU.add),
                                     deps=[g], sdeps=[w_ready[t]] + (eb2_added[t] or []), inc=vE)
                        evs.append(st)
                        banks.release([bl[n]], [st])
                    last_eo_group = eo_groups[-1]
                    if e_ == NE - 1:
                        store_q.append((t, list(evs)))
                eht_war = eo_groups[-2:]
                if e_ == NE - 1:
                    cevs = [ms for _, evs2 in store_q[-4:] for ms in evs2]
                    flush = em.emit("vector", lambda e: e.memset(stat[:, 65:66], 0.0),
                                    sdeps=cevs + [flush_prev["vector"]], inc=vE)
                    flush_prev["vector"] = flush
                    dst = d_out[b].rearrange("(t p) d -> p t d", p=128)[:, 4 * c:4 * c + 4, :]
                    om = em.emit("scalar",
                                 lambda e, dst=dst, c=c: e.dma_start(
                                     out=dst, in_=out1[:, 4 * c:4 * c + 4, :]),
                                 deps=[flush] + cevs, inc=dmaX, dma=True)
                    out_dma_ms.append(om)
            if _DBG and b == 0 and e_ in (0, 3):
                dst = d_dbg3 if e_ == 0 else d_dbg4
                fl = dbg_flush([evs])
                em.emit("scalar", lambda e, dst=dst: e.dma_start(out=dst, in_=out1[:]),
                        deps=[fl], inc=dmaX, dma=True)
            if e_ < NE - 1:
                st_done["EW2"] = em.dma_total(dmaG)
                ew1_pair = (next_ew1_ms, next_ew1_ms)
                ew2_ms = wload(b, w_chunks(d_ew2[e_ + 1], EW2, HT, D),
                               [last_eo_group, st_done.get("EW2")],
                               sc_e2[e_ + 1], EW2[:].rearrange("p a b -> p (a b)"),
                               sc_slices=sc_slices_w(sc_e2[e_ + 1], HT, D), ring=4)
            ch_free = [last_eh_group]
            ew2_free = [last_eo_group]
        return None

    r = emit_batch(0)
    if r != "stop":
        em.barrier(resources=(banks,))
        emit_batch(1)
    em.barrier(resources=(banks,))

    with nc.Block() as blk:
        em.finish(blk)
    return nc


_CACHED = None


def _get_program():
    global _CACHED
    if _CACHED is None:
        _CACHED = build_program()
    return _CACHED


def kernel(L0_attn_y_all, L0_attn_Wq, L0_attn_bq, L0_attn_Wk, L0_attn_bk,
           L0_attn_Wv, L0_attn_bv, L0_attn_Wo, L0_attn_bo,
           L0_ffn_rw1, L0_ffn_rb1, L0_ffn_rw2, L0_ffn_rb2,
           L0_ffn_ew1, L0_ffn_eb1, L0_ffn_ew2, L0_ffn_eb2):
    nc = _get_program()
    f = np.float32
    x = np.ascontiguousarray(np.asarray(L0_attn_y_all, dtype=f))
    bq = np.asarray(L0_attn_bq, f).reshape(8, 128)
    bk = np.asarray(L0_attn_bk, f).reshape(8, 128)
    rb1 = np.asarray(L0_ffn_rb1, f).reshape(8, 128)
    eb1 = np.asarray(L0_ffn_eb1, f).reshape(NE, 16, 128)
    rw2 = np.asarray(L0_ffn_rw2, f).reshape(8, 128, 8)
    biaspack = np.zeros((128, 1280), dtype=f)
    biaspack[:, 0:1024] = np.asarray(L0_attn_bv, f).reshape(1, D)
    biaspack[:, 1024:1032] = bq.T
    biaspack[:, 1032:1040] = bk.T
    biaspack[:, 1040:1048] = rb1.T
    biaspack[:, 1048:1176] = eb1.transpose(2, 0, 1).reshape(128, 128)
    biaspack[:, 1176:1240] = rw2.transpose(1, 0, 2).reshape(128, 64)
    rowpack = np.zeros((128, 1032), dtype=f)
    rowpack[0:8, 0:1024] = np.asarray(L0_ffn_eb2, f)
    rowpack[8, 0:1024] = np.asarray(L0_attn_bo, f)
    rowpack[0, 1024:1032] = np.asarray(L0_ffn_rb2, f)
    base = {
        "wq": np.ascontiguousarray(np.asarray(L0_attn_Wq, f)),
        "wk": np.ascontiguousarray(np.asarray(L0_attn_Wk, f)),
        "wv": np.ascontiguousarray(np.asarray(L0_attn_Wv, f)),
        "wo": np.ascontiguousarray(np.asarray(L0_attn_Wo, f)),
        "rw1": np.ascontiguousarray(np.asarray(L0_ffn_rw1, f)),
        "ew1": np.ascontiguousarray(np.asarray(L0_ffn_ew1, f)),
        "ew2": np.ascontiguousarray(np.asarray(L0_ffn_ew2, f)),
        "idbf": np.eye(128, dtype=ml_dtypes.bfloat16),
        "tri01": np.tril(np.ones((128, 128), dtype=ml_dtypes.bfloat16)),
        "biaspack": biaspack,
        "rowpack": rowpack,
    }
    in_maps = []
    for c in range(NCORES):
        m = dict(base)
        m["x"] = np.ascontiguousarray(x[2 * c:2 * c + 2])
        in_maps.append(m)
    res = run_bass_kernel_spmd(nc, in_maps, core_ids=list(range(NCORES)))
    out = np.concatenate([res.results[c]["out"] for c in range(NCORES)], axis=0)
    if _DBG:
        kernel._dbg = [res.results[c] for c in range(NCORES)]
    return out.reshape(16, T, D)


# version 23
# speedup vs baseline: 1.0139x; 1.0139x over previous
import numpy as np
import ml_dtypes
import concourse.bass as bass
import concourse.mybir as mybir
from concourse.bass_utils import run_bass_kernel_spmd

F32 = mybir.dt.float32
BF16 = mybir.dt.bfloat16
AF = mybir.ActivationFunctionType
ALU = mybir.AluOpType
AX = mybir.AxisListType

NCORES = 8
NB = 2
T = 2048
D = 1024
NT = 16
DT = 8
NE = 8
H = 2048
HT = 16

ENGS = ("sync", "tensor", "vector", "scalar", "gpsimd")

_DBG = 0
_STOP = None


class Sem:
    __slots__ = ("h", "n", "key")

    def __init__(self, h, key):
        self.h = h
        self.n = 0
        self.key = key


class Mst:
    __slots__ = ("sem", "val", "eng", "idx")

    def __init__(self, sem, val, eng, idx):
        self.sem = sem
        self.val = val
        self.eng = eng
        self.idx = idx


class SemPool:

    def __init__(self, sems):
        self.sems = sems
        self.rr = 0

    def next(self):
        s = self.sems[self.rr % len(self.sems)]
        self.rr += 1
        return s


def _flat(deps):
    out = []
    for d in deps:
        if d is None:
            continue
        if isinstance(d, (list, tuple)):
            out.extend(_flat(d))
        else:
            out.append(d)
    return out


class EM:

    def __init__(self, nc):
        self.nc = nc
        self.q = {e: [] for e in ENGS}
        self.waited = {e: {} for e in ENGS}
        self.idx = 0
        self.floor = 0
        self.bar = self.sem("bar")
        self.last_dma = {}
        self.sync_dmas = []
        self.scalar_dmas = []

    def sem(self, name):
        return Sem(self.nc.alloc_semaphore(name), name)

    def sempool(self, name, n):
        return SemPool([self.sem(f"{name}{i}") for i in range(n)])

    def emit(self, eng, build, deps=(), inc=None, dma=False, sdeps=()):
        self.idx += 1
        amt = 16 if dma else 1
        ms = None
        if isinstance(inc, SemPool):
            inc = inc.next()
        if inc is not None:
            inc.n += amt
            ms = Mst(inc, inc.n, None if dma else eng, self.idx)
            if dma:
                self.last_dma[inc.key] = ms
        wl = {}
        for d, strict in [(x, False) for x in _flat(deps)] + [(x, True) for x in _flat(sdeps)]:
            if d.idx <= self.floor:
                continue
            if d.eng == eng and not strict:
                continue
            prev = self.waited[eng].get(d.sem.key, 0)
            if d.val <= prev:
                continue
            cur = wl.get(d.sem.key)
            if cur is None or d.val > cur[1]:
                wl[d.sem.key] = (d.sem.h, d.val)
        for k, (h, v) in wl.items():
            self.waited[eng][k] = max(self.waited[eng].get(k, 0), v)
        self.q[eng].append((list(wl.values()), build, inc.h if inc is not None else None, amt))
        return ms

    def dma_total(self, sem):
        if isinstance(sem, SemPool):
            return [self.last_dma.get(s.key) for s in sem.sems]
        return self.last_dma.get(sem.key)

    def barrier(self, resources=()):
        self.bar.n += len(ENGS)
        tgt = self.bar.n
        for e in ENGS:
            if e == "sync":
                deps = [self.dma_total(s) for s in self.sync_dmas]
            elif e == "scalar":
                deps = [self.dma_total(s) for s in self.scalar_dmas]
            elif e == "gpsimd":
                deps = [self.dma_total(s) for s in getattr(self, "gpsimd_dmas", [])]
            else:
                deps = []

            def build(eng, tgt=tgt, e=e):
                if e in ("vector", "scalar", "tensor"):
                    eng.drain()
                eng.sem_inc(self.bar.h, 1)
                return eng.wait_ge(self.bar.h, tgt)

            self.emit(e, build, deps=deps)
        self.floor = self.idx
        for r in resources:
            r.reset()

    def finish(self, blk):
        def runner(name):
            def go(eng):
                for waits, build, inc_h, amt in self.q[name]:
                    for h, v in waits:
                        eng.wait_ge(h, v)
                    ins = build(eng)
                    if inc_h is not None:
                        ins.then_inc(inc_h, amt)
            return go

        blk.sync(runner("sync"))
        blk.tensor(runner("tensor"))
        blk.vector(runner("vector"))
        blk.scalar(runner("scalar"))
        blk.gpsimd(runner("gpsimd"))


class Banks:

    def __init__(self, pall):
        self.pall = pall
        self.free = [[] for _ in range(8)]
        self.busy = [False] * 8
        self.rr = 0

    def reset(self):
        assert not any(self.busy), "psum bank busy across barrier"
        self.free = [[] for _ in range(8)]
        self.busy = [False] * 8
        self.rr = 0

    def f32(self, k):
        return self.pall[:, k * 512:(k + 1) * 512]

    def bf16(self, k):
        return self.pall[:, k * 512:(k + 1) * 512].bitcast(BF16)

    def acquire(self, n=1, which=None):
        if which is None:
            which = []
            for _ in range(n):
                got = None
                for off in range(8):
                    s = (self.rr + off) % 8
                    if not self.busy[s]:
                        got = s
                        self.rr = (s + 1) % 8
                        break
                assert got is not None, "no free psum bank"
                which.append(got)
        deps = []
        for b in which:
            assert not self.busy[b], f"psum bank {b} busy"
            self.busy[b] = True
            deps.extend(self.free[b])
        return which, deps

    def release(self, which, ms):
        ms = _flat([ms])
        for b in which:
            assert self.busy[b]
            self.busy[b] = False
            self.free[b] = list(ms)


def align32(x):
    return (x + 31) & ~31


def build_program():
    nc = bass.Bass("TRN2", target_bir_lowering=False)

    d_x = nc.dram_tensor("x", [NB, T, D], F32, kind="ExternalInput").ap()
    d_wq = nc.dram_tensor("wq", [D, D], F32, kind="ExternalInput").ap()
    d_wk = nc.dram_tensor("wk", [D, D], F32, kind="ExternalInput").ap()
    d_wv = nc.dram_tensor("wv", [D, D], F32, kind="ExternalInput").ap()
    d_wo = nc.dram_tensor("wo", [D, D], F32, kind="ExternalInput").ap()
    d_rw1 = nc.dram_tensor("rw1", [D, D], F32, kind="ExternalInput").ap()
    d_ew1 = nc.dram_tensor("ew1", [NE, D, H], F32, kind="ExternalInput").ap()
    d_ew2 = nc.dram_tensor("ew2", [NE, H, D], F32, kind="ExternalInput").ap()
    d_idbf = nc.dram_tensor("idbf", [128, 128], BF16, kind="ExternalInput").ap()
    d_tri01 = nc.dram_tensor("tri01", [128, 128], BF16, kind="ExternalInput").ap()
    d_bias = nc.dram_tensor("biaspack", [128, 1280], F32, kind="ExternalInput").ap()
    d_rows = nc.dram_tensor("rowpack", [128, 1032], F32, kind="ExternalInput").ap()
    d_out = nc.dram_tensor("out", [NB, T, D], F32, kind="ExternalOutput").ap()
    if _DBG:
        d_dbg1 = nc.dram_tensor("dbg1", [128, NT, D], F32, kind="ExternalOutput").ap()
        d_dbg2 = nc.dram_tensor("dbg2", [128, 128], F32, kind="ExternalOutput").ap()
        d_dbgQ = nc.dram_tensor("dbgQ", [128, DT, T], BF16, kind="ExternalOutput").ap()
        d_dbgK = nc.dram_tensor("dbgK", [128, DT, T], BF16, kind="ExternalOutput").ap()
        d_dbgV = nc.dram_tensor("dbgV", [128, NT, D], BF16, kind="ExternalOutput").ap()
        d_dbgS = nc.dram_tensor("dbgS", [128, 68], F32, kind="ExternalOutput").ap()
        d_dbg3 = nc.dram_tensor("dbg3", [128, NT, D], F32, kind="ExternalOutput").ap()
        d_dbg4 = nc.dram_tensor("dbg4", [128, NT, D], F32, kind="ExternalOutput").ap()

    sc_w = {k: nc.dram_tensor(f"sc_{k}", [128, DT * D], BF16, kind="Internal").ap()
            for k in ("wk", "wv", "wq", "wo", "rw1")}
    sc_e1 = nc.dram_tensor("sc_e1", [NE, 128, DT * H], BF16, kind="Internal").ap()
    sc_e2 = nc.dram_tensor("sc_e2", [NE, 128, HT * D], BF16, kind="Internal").ap()

    cur = align32(nc.sbuf_base)

    def alloc(name, shape, dtype, at=None):
        nonlocal cur
        off = cur if at is None else at
        h = nc.alloc_sbuf_tensor_at(name, shape, dtype, offset=off)
        per = 1
        for s in shape[1:]:
            per *= s
        end = align32(off + per * mybir.dt.size(dtype))
        if at is None:
            cur = end
        else:
            cur = max(cur, end)
        return h

    ident_bf = alloc("ident_bf", [128, 128], BF16)
    tri01 = alloc("tri01", [128, 128], BF16)
    ones_bf = alloc("ones_bf", [1, 128], BF16)
    bqcol = alloc("bqcol", [128, 8], F32)
    bkcol = alloc("bkcol", [128, 8], F32)
    rb1col = alloc("rb1col", [128, 8], F32)
    eb1cols = alloc("eb1cols", [128, 128], F32)
    bv_bc = alloc("bv_bc", [128, D], BF16)
    rb2row = alloc("rb2row", [1, 8], BF16)
    ebplus = alloc("ebplus", [9, D], BF16)
    rw2sb = alloc("rw2sb", [128, 8, 8], BF16)
    w_sb = alloc("w_sb", [128, 128], F32)
    wbf_sb = alloc("wbf_sb", [128, 9], BF16)
    rsc = alloc("rsc", [128, 6, 8], F32)
    lnstats = alloc("lnstats", [128, 32], F32)
    stat = alloc("stat", [128, 68], F32)
    cv_f = alloc("cv_f", [128, 2, 1024], F32)
    out1 = alloc("out1", [128, NT, D], F32)

    r1 = cur
    BQT = alloc("BQT", [128, DT, T], BF16, at=r1)
    LN2T = alloc("LN2T", [128, DT, T], BF16, at=r1)
    r2 = cur
    KT = alloc("KT", [128, DT, T], BF16, at=r2)
    CH = alloc("CH", [128, DT, T], BF16, at=r2)
    r3 = cur
    V = alloc("V", [128, NT, D], BF16, at=r3)
    EW2 = alloc("EW2", [128, HT, D], BF16, at=r3)
    r4 = cur
    WS = alloc("WS", [128, DT, D], BF16, at=r4)
    EHT = alloc("EHT", [128, HT, 512], BF16, at=r4)
    wT_sb = alloc("wT_sb", [9, NT * 128], BF16)
    ln_bf = alloc("ln_bf", [128, 2, D], BF16)
    ppbuf = alloc("ppbuf", [128, T], BF16)
    ctxT_sb = alloc("ctxT_sb", [128, D], BF16)

    assert cur <= nc.sbuf_top, f"SBUF overflow: {cur} > {nc.sbuf_top}"

    pall = nc.alloc_psum_tensor("pall", [128, 4096], F32)

    em = EM(nc)
    tE = em.sem("tE")
    vE = em.sem("vE")
    sE = em.sem("sE")
    dmaW = em.sempool("dmaW", 4)
    dmaX = em.sempool("dmaX", 4)
    dmaG = em.sem("dmaG")
    em.sync_dmas = list(dmaW.sems)
    em.scalar_dmas = list(dmaX.sems)
    em.gpsimd_dmas = [dmaG]
    banks = Banks(pall[:])

    def S_(base, col, w=1):
        return stat[:, base + col: base + col + w]

    EPS_COL = 64

    def group(deps, psum_ap, ops, inc=tE):
        ms = None
        n = len(ops)
        for k, (l, r) in enumerate(ops):
            ms = em.emit(
                "tensor",
                lambda e, o=psum_ap, l=l, r=r, st=(k == 0), sp=(k == n - 1):
                    e.matmul(o, l, r, start=st, stop=sp),
                deps=deps if k == 0 else (),
                inc=inc if k == n - 1 else None)
        return ms

    stg_rr = [0]
    stg_free = [[], [], [], []]
    cvslots = [None, None, None, None]
    flush_prev = {"vector": None, "scalar": None}

    def stage_cast(chunks, war_deps, eng="vector", scratch=None, ring=2):
        ms = None
        pend_store = None
        for idx, (srcap, dstap) in enumerate(chunks):
            s = stg_rr[0] % ring
            stg_rr[0] += 1
            slot = cvslots[s]
            ld = em.emit("sync", lambda e, d=slot, sr=srcap: e.dma_start(out=d, in_=sr),
                         deps=stg_free[s], inc=dmaW, dma=True)
            if eng == "vector":
                ms = em.emit("vector",
                             lambda e, d=dstap, s2=slot: e.tensor_copy(out=d, in_=s2),
                             deps=[ld] + (war_deps if idx == 0 else []), inc=vE)
            else:
                ms = em.emit("scalar",
                             lambda e, d=dstap, s2=slot: e.activation(out=d, in_=s2,
                                                                      func=AF.Identity),
                             deps=[ld] + (war_deps if idx == 0 else []), inc=sE)
            stg_free[s] = [ms]
            if scratch is not None:
                if pend_store is not None:
                    sc_ap, sb_ap = pend_store
                    em.emit("gpsimd", lambda e, d=sc_ap, sr=sb_ap: e.dma_start(out=d, in_=sr),
                            deps=[ms], inc=dmaG, dma=True)
                pend_store = (scratch[idx], dstap)
        fl = em.emit(eng, (lambda e: e.memset(stat[:, 66:67], 0.0)) if eng == "vector"
                     else (lambda e: e.activation(out=stat[:, 67:68], in_=stat[:, 67:68],
                                                  func=AF.Identity)),
                     sdeps=[ms, flush_prev[eng]], inc=vE if eng == "vector" else sE)
        flush_prev[eng] = fl
        if scratch is not None and pend_store is not None:
            sc_ap, sb_ap = pend_store
            em.emit("gpsimd", lambda e, d=sc_ap, sr=sb_ap: e.dma_start(out=d, in_=sr),
                    deps=[fl], inc=dmaG, dma=True)
        return fl

    def stage_stepper(chunks, war_deps, eng="vector", scratch=None, ring=2):
        st = {"ms": None, "pend": None, "idx": 0}

        def step():
            idx = st["idx"]
            srcap, dstap = chunks[idx]
            s = stg_rr[0] % ring
            stg_rr[0] += 1
            slot = cvslots[s]
            ld = em.emit("sync", lambda e, d=slot, sr=srcap: e.dma_start(out=d, in_=sr),
                         deps=stg_free[s], inc=dmaW, dma=True)
            if eng == "vector":
                ms = em.emit("vector", lambda e, d=dstap, s2=slot: e.tensor_copy(out=d, in_=s2),
                             deps=[ld] + (war_deps if idx == 0 else []), inc=vE)
            else:
                ms = em.emit("scalar", lambda e, d=dstap, s2=slot: e.activation(
                    out=d, in_=s2, func=AF.Identity),
                             deps=[ld] + (war_deps if idx == 0 else []), inc=sE)
            stg_free[s] = [ms]
            if scratch is not None:
                if st["pend"] is not None:
                    sc_ap, sb_ap = st["pend"]
                    em.emit("gpsimd", lambda e, d=sc_ap, sr=sb_ap: e.dma_start(out=d, in_=sr),
                            deps=[ms], inc=dmaG, dma=True)
                st["pend"] = (scratch[idx], dstap)
            st["ms"] = ms
            st["idx"] += 1

        def fin():
            fl = em.emit(eng, (lambda e: e.memset(stat[:, 66:67], 0.0)) if eng == "vector"
                         else (lambda e: e.activation(out=stat[:, 67:68], in_=stat[:, 67:68],
                                                      func=AF.Identity)),
                         sdeps=[st["ms"], flush_prev[eng]], inc=vE if eng == "vector" else sE)
            flush_prev[eng] = fl
            if scratch is not None and st["pend"] is not None:
                sc_ap, sb_ap = st["pend"]
                em.emit("gpsimd", lambda e, d=sc_ap, sr=sb_ap: e.dma_start(out=d, in_=sr),
                        deps=[fl], inc=dmaG, dma=True)
            return fl
        return step, fin

    def wload(b, chunks, war_deps, sc_whole, dst_whole, eng="vector", sc_slices=None, ring=2):
        if b == 0:
            return stage_cast(chunks, war_deps, eng=eng, scratch=sc_slices, ring=ring)
        return em.emit("sync", lambda e, d=dst_whole, sr=sc_whole: e.dma_start(out=d, in_=sr),
                       deps=war_deps, inc=dmaW, dma=True)

    def sc_slices_w(sc, rows, width):
        out = []
        for r in range(rows):
            for h2 in range(width // 1024):
                out.append(sc[:, r * width + h2 * 1024: r * width + (h2 + 1) * 1024])
        return out

    def w_chunks(dsrc, dst, rows, width):
        out = []
        for r in range(rows):
            for h2 in range(width // 1024):
                out.append((dsrc[r * 128:(r + 1) * 128, h2 * 1024:(h2 + 1) * 1024],
                            dst[:, r, h2 * 1024:(h2 + 1) * 1024]))
        return out

    cv_flat = cv_f[:].rearrange("p a b -> p (a b)")
    bstage = cv_flat[:, 0:1280]
    rstage = cv_flat[:, 0:1032]
    l1 = em.emit("scalar", lambda e: e.dma_start(out=bstage, in_=d_bias), inc=dmaX, dma=True)
    em.emit("scalar", lambda e: e.dma_start(out=ident_bf[:], in_=d_idbf), inc=dmaX, dma=True)
    em.emit("scalar", lambda e: e.dma_start(out=tri01[:], in_=d_tri01), inc=dmaX, dma=True)
    tot_x = em.dma_total(dmaX)

    em.emit("vector", lambda e: e.memset(ones_bf[:], 1.0), inc=vE)
    em.emit("vector", lambda e: e.memset(wbf_sb[:, 8:9], 1.0), inc=vE)
    eps_ms = em.emit("vector", lambda e: e.memset(stat[:, EPS_COL:EPS_COL + 1], 1e-5), inc=vE)
    ones_ms = em.emit("vector", lambda e: e.memset(stat[:, EPS_COL + 1:EPS_COL + 2], 0.0), inc=vE)
    flush_prev["scalar"] = em.emit("scalar",
                                   lambda e: e.activation(out=stat[:, 67:68],
                                                          in_=stat[:, EPS_COL:EPS_COL + 1],
                                                          func=AF.Identity),
                                   deps=[eps_ms], inc=sE)

    bias_dep = [ones_ms]
    for fn in (
        lambda e: e.tensor_copy(out=bv_bc[:], in_=bstage[:, 0:1024]),
        lambda e: e.tensor_copy(out=bqcol[:], in_=bstage[:, 1024:1032]),
        lambda e: e.tensor_copy(out=bkcol[:], in_=bstage[:, 1032:1040]),
        lambda e: e.tensor_copy(out=rb1col[:], in_=bstage[:, 1040:1048]),
        lambda e: e.tensor_copy(out=eb1cols[:], in_=bstage[:, 1048:1176]),
        lambda e: e.tensor_copy(out=rw2sb[:].rearrange("p a b -> p (a b)"), in_=bstage[:, 1176:1240]),
    ):
        bias_dep.append(em.emit("vector", fn, deps=[l1, tot_x], inc=vE))
    rw2c = bias_dep[-1]
    l2 = em.emit("scalar", lambda e: e.dma_start(out=rstage, in_=d_rows),
                 deps=list(bias_dep), inc=dmaX, dma=True)
    tot_x2 = em.dma_total(dmaX)
    for fn in (
        lambda e: e.tensor_copy(out=ebplus[:], in_=rstage[0:9, 0:1024]),
        lambda e: e.tensor_copy(out=rb2row[:], in_=rstage[0:1, 1024:1032]),
    ):
        bias_dep.append(em.emit("vector", fn, deps=[tot_x2], inc=vE))
    P0DONE = list(bias_dep) + [tot_x2]
    cvslots[0] = cv_f[:, 0, :]
    cvslots[1] = cv_f[:, 1, :]
    cvslots[2] = ppbuf[:].bitcast(F32)
    cvslots[3] = ln_bf[:].rearrange("p a b -> p (a b)").bitcast(F32)
    stg_free[0] = list(P0DONE)
    stg_free[1] = list(P0DONE)

    ws_free = []
    st_done = {}
    ch_free = []
    ew2_free = []
    eht_war = []
    out_dma_ms = []

    def ln_stats(t, src_ap, src_deps, ln_prev, rstd_ap, nmrs_ap, sq_scratch, sq_war=(),
                 src_sdeps=()):
        base = (t % 4) * 16
        s1 = em.emit("vector", lambda e, base=base: e.reduce_sum(out=S_(base, 0), in_=src_ap, axis=AX.X),
                     deps=src_deps, sdeps=src_sdeps, inc=vE)
        sq = em.emit("vector",
                     lambda e, base=base: e.scalar_tensor_tensor(
                         out=sq_scratch, in0=src_ap, scalar=1.0,
                         in1=src_ap, op0=ALU.mult, op1=ALU.mult,
                         accum_out=S_(base, 1)),
                     deps=list(src_deps) + list(sq_war), sdeps=src_sdeps, inc=vE)
        msq = em.emit("vector", lambda e, base=base: e.scalar_tensor_tensor(
            out=S_(base, 3), in0=S_(base, 0), scalar=1.0 / (D * D), in1=S_(base, 0),
            op0=ALU.mult, op1=ALU.mult), sdeps=[s1], inc=vE)
        var = em.emit("vector", lambda e, base=base: e.tensor_scalar(out=S_(base, 4), in0=S_(base, 1),
                                                                    scalar1=1.0 / D, scalar2=S_(base, 3),
                                                                    op0=ALU.mult, op1=ALU.subtract),
                      deps=[ln_prev[t % 4]], sdeps=[sq, msq], inc=vE)
        lnv = em.emit("scalar", lambda e, base=base: e.activation(out=S_(base, 5), in_=S_(base, 4),
                                                                  func=AF.Ln,
                                                                  bias=stat[:, EPS_COL:EPS_COL + 1]),
                      deps=[var], inc=sE)
        ln_prev[t % 4] = lnv
        rstd = em.emit("scalar", lambda e, d=rstd_ap, base=base: e.activation(out=d, in_=S_(base, 5),
                                                                              func=AF.Exp,
                                                                              scale=-0.5), sdeps=[lnv], inc=sE)
        nmrs = em.emit("vector", lambda e, d=nmrs_ap, r=rstd_ap, base=base: e.scalar_tensor_tensor(
            out=d, in0=S_(base, 0), scalar=-1.0 / D, in1=r,
            op0=ALU.mult, op1=ALU.mult),
                       deps=[rstd], sdeps=[s1], inc=vE)
        return nmrs, rstd

    def ln_stage1(t, src_ap, src_deps, sq_war):
        base = (t % 4) * 16
        lb = ln_bf[:, t % 2, :]
        s1 = em.emit("vector", lambda e, base=base, src_ap=src_ap: e.reduce_sum(
            out=S_(base, 0), in_=src_ap, axis=AX.X), deps=src_deps, inc=vE)
        sq = em.emit("vector",
                     lambda e, base=base, src_ap=src_ap, lb=lb: e.scalar_tensor_tensor(
                         out=lb, in0=src_ap, scalar=1.0, in1=src_ap,
                         op0=ALU.mult, op1=ALU.mult, accum_out=S_(base, 1)),
                     deps=list(src_deps) + list(sq_war), inc=vE)
        return s1, sq

    def ln_stage2(t, s1, sq, ln_prev):
        base = (t % 4) * 16
        msq = em.emit("vector", lambda e, base=base: e.scalar_tensor_tensor(
            out=S_(base, 3), in0=S_(base, 0), scalar=1.0 / (D * D), in1=S_(base, 0),
            op0=ALU.mult, op1=ALU.mult), sdeps=[s1], inc=vE)
        var = em.emit("vector", lambda e, base=base: e.tensor_scalar(out=S_(base, 4), in0=S_(base, 1),
                                                                    scalar1=1.0 / D, scalar2=S_(base, 3),
                                                                    op0=ALU.mult, op1=ALU.subtract),
                      deps=[ln_prev[t % 4]], sdeps=[sq, msq], inc=vE)
        lnv = em.emit("scalar", lambda e, base=base: e.activation(out=S_(base, 5), in_=S_(base, 4),
                                                                  func=AF.Ln,
                                                                  bias=stat[:, EPS_COL:EPS_COL + 1]),
                      deps=[var], inc=sE)
        ln_prev[t % 4] = lnv
        rstd = em.emit("scalar", lambda e, base=base: e.activation(out=S_(base, 6), in_=S_(base, 5),
                                                                   func=AF.Exp,
                                                                   scale=-0.5), sdeps=[lnv], inc=sE)
        nmrs = em.emit("vector", lambda e, base=base: e.scalar_tensor_tensor(
            out=S_(base, 7), in0=S_(base, 0), scalar=-1.0 / D, in1=S_(base, 6),
            op0=ALU.mult, op1=ALU.mult),
                       deps=[rstd], sdeps=[s1], inc=vE)
        return nmrs, rstd

    def ln_finish(t, src_ap, rstd_ap, nmrs_ap, stats_ms, dst_dep):
        lb = ln_bf[:, t % 2, :]
        norm = em.emit("scalar",
                       lambda e, lb=lb, r=rstd_ap, nm=nmrs_ap: e.activation(
                           out=lb, in_=src_ap, func=AF.Identity,
                           bias=nm, scale=r),
                       deps=[stats_ms[0]] + dst_dep["lnbf_war"][t % 2],
                       sdeps=[stats_ms[1]], inc=sE)
        bnk, bd = banks.acquire(1)
        trs = None
        for k in range(8):
            trs = em.emit("tensor",
                          lambda e, k=k, b=bnk[0], lb=lb: e.transpose(
                              banks.bf16(b)[:, k * 128:(k + 1) * 128],
                              lb[:, k * 128:(k + 1) * 128], ident_bf[:]),
                          deps=([norm] + bd + dst_dep.get("tr_dep", [])) if k == 0 else (),
                          inc=tE if k == 7 else None)
        dst = dst_dep["dstT"]
        ev = em.emit("vector",
                     lambda e, b=bnk[0]: e.tensor_copy(
                         out=dst[:, :, t * 128:(t + 1) * 128],
                         in_=banks.bf16(b).rearrange("p (a q) -> p a q", a=8)),
                     deps=[trs] + dst_dep["dst_war"], inc=vE)
        banks.release(bnk, [ev])
        return ev, norm, trs

    def ln_tile(tag, t, src_ap, dst_dep, b):
        base = (t % 4) * 16
        lb = ln_bf[:, t % 2, :]
        stats = ln_stats(t, src_ap, dst_dep["src_ready"], dst_dep["ln_prev"],
                         S_(base, 6), S_(base, 7), lb, sq_war=dst_dep["lnbf_war"][t % 2])
        return ln_finish(t, src_ap, S_(base, 6), S_(base, 7), stats, dst_dep)

    def emit_batch(b):
        nonlocal ws_free, ch_free, ew2_free, eht_war, out_dma_ms
        lnT_ready = [None] * NT
        lnbf_war = [[], []]
        ln_prev = [None, None, None, None]

        wk_step = wk_fin = None
        if b == 0:
            wk_step, wk_fin = stage_stepper(w_chunks(d_wk, WS, DT, D),
                                            ws_free + [st_done.get("WS")],
                                            scratch=sc_slices_w(sc_w["wk"], DT, D))
            wk_ld = None
        else:
            wk_ld = em.emit("sync", lambda e: e.dma_start(
                out=WS[:].rearrange("p a b -> p (a b)"), in_=sc_w["wk"]),
                deps=ws_free + [st_done.get("WS")], inc=dmaW, dma=True)

        kT_ready = [None] * 4
        last_k_group = None

        def emit_kT_chunk(c):
            nonlocal last_k_group
            cdep = [lnT_ready[4 * c + i] for i in range(4)]
            for a in range(8):
                bnk, bd = banks.acquire(1)
                g = group(cdep + bd + [wk_ld],
                          banks.f32(bnk[0]),
                          [(WS[:, d, a * 128:(a + 1) * 128], BQT[:, d, c * 512:(c + 1) * 512])
                           for d in range(DT)])
                ev = em.emit("scalar",
                             lambda e, a=a, c=c, b_=bnk[0]: e.activation(
                                 out=KT[:, a, c * 512:(c + 1) * 512], in_=banks.f32(b_),
                                 func=AF.Identity, bias=bkcol[:, a:a + 1]),
                             deps=[g], inc=sE)
                banks.release(bnk, [ev])
                last_k_group = g
                kT_ready[c] = ev

        def finish_tile(tp, s1p, sqp):
            stats = ln_stage2(tp, s1p, sqp, ln_prev)
            base_p = (tp % 4) * 16
            dd = {"lnbf_war": lnbf_war, "dstT": BQT, "dst_war": [],
                  "tr_dep": (P0DONE if (b == 0 and tp == 0) else [])}
            ev, norm, trs = ln_finish(tp, out1[:, tp, :], S_(base_p, 6), S_(base_p, 7),
                                      stats, dd)
            lnT_ready[tp] = ev
            lnbf_war[tp % 2] = [trs]

        x_ld = [None] * NT
        pend = None
        ksched = {7: 0, 11: 1, 15: 2} if b == 0 else {4: 0, 8: 1, 12: 2}
        for t in range(NT):
            p0 = P0DONE if (b == 0 and t == 0) else []
            x_ld[t] = em.emit("scalar",
                              lambda e, t=t: e.dma_start(out=out1[:, t, :],
                                                         in_=d_x[b, t * 128:(t + 1) * 128, :]),
                              deps=p0 + ([x_ld[t - 4]] if t >= 4 else []), inc=dmaX, dma=True)
            if wk_step is not None and t < 8:
                wk_step()
                if t == 7:
                    wk_ld = wk_fin()
            s1, sq = ln_stage1(t, out1[:, t, :], p0 + [x_ld[t]], lnbf_war[t % 2])
            if pend is not None:
                finish_tile(*pend)
            pend = (t, s1, sq)
            c = ksched.get(t)
            if c is not None:
                emit_kT_chunk(c)
        finish_tile(*pend)
        emit_kT_chunk(3)


        def dbg_flush(deps):
            f1 = em.emit("vector", lambda e: e.memset(stat[:, 67:68], 0.0), deps=deps,
                         sdeps=deps, inc=vE)
            return em.emit("scalar", lambda e: e.activation(out=stat[:, 67:68], in_=stat[:, 67:68],
                                                            func=AF.Identity),
                           deps=[f1], inc=sE)

        if _DBG and _STOP == "ln" and b == 0:
            fl = dbg_flush(lnT_ready)
            em.emit("scalar", lambda e: e.dma_start(out=d_dbgQ, in_=BQT[:]),
                    deps=[fl], inc=dmaX, dma=True)
            em.emit("scalar", lambda e: e.dma_start(out=d_dbgS, in_=stat[:]),
                    deps=[fl], inc=dmaX, dma=True)
            em.emit("scalar", lambda e: e.dma_start(out=d_dbg1, in_=out1[:]),
                    deps=[fl], inc=dmaX, dma=True)
            return "stop"
        st_done["WS"] = em.dma_total(dmaG)
        wv_ld = wload(b, w_chunks(d_wv, WS, DT, D), [last_k_group, st_done.get("WS")], sc_w["wv"],
                      WS[:].rearrange("p a b -> p (a b)"), sc_slices=sc_slices_w(sc_w["wv"], DT, D))
        v_ready = [None] * NT
        last_v_group = None
        for c in range(4):
            for t in range(4 * c, 4 * c + 4):
                for n in range(2):
                    bnk, bd = banks.acquire(1)
                    g = group([lnT_ready[t], wv_ld] + bd,
                              banks.f32(bnk[0]),
                              [(BQT[:, d, t * 128:(t + 1) * 128], WS[:, d, n * 512:(n + 1) * 512])
                               for d in range(DT)])
                    ev = em.emit("vector",
                                 lambda e, t=t, n=n, b_=bnk[0]: e.tensor_tensor(
                                     out=V[:, t, n * 512:(n + 1) * 512], in0=banks.f32(b_),
                                     in1=bv_bc[:, n * 512:(n + 1) * 512], op=ALU.add),
                                 deps=[g], inc=vE)
                    banks.release(bnk, [ev])
                    last_v_group = g
                    v_ready[t] = ev
        st_done["WS"] = em.dma_total(dmaG)
        wq_ld = wload(b, w_chunks(d_wq, WS, DT, D), [last_v_group, st_done.get("WS")], sc_w["wq"],
                      WS[:].rearrange("p a b -> p (a b)"), sc_slices=sc_slices_w(sc_w["wq"], DT, D))
        qT_ready = [None] * 4
        last_q_group = None
        for c in range(4):
            bl, bd = banks.acquire(8)
            gms = []
            for a in range(8):
                g = group((bd + [wq_ld]) if a == 0 else [],
                          banks.f32(bl[a]),
                          [(WS[:, d, a * 128:(a + 1) * 128], BQT[:, d, c * 512:(c + 1) * 512])
                           for d in range(DT)])
                gms.append(g)
            evs = []
            for a in range(8):
                ev = em.emit("scalar",
                             lambda e, a=a, c=c, b_=bl[a]: e.activation(
                                 out=BQT[:, a, c * 512:(c + 1) * 512], in_=banks.f32(b_),
                                 func=AF.Identity, bias=bqcol[:, a:a + 1]),
                             deps=[gms[-1]], inc=sE)
                evs.append(ev)
                banks.release([bl[a]], [ev])
            last_q_group = gms[-1]
            qT_ready[c] = evs[-1]

        if _DBG and _STOP == "p1" and b == 0:
            fl = dbg_flush(qT_ready + kT_ready + v_ready)
            em.emit("scalar", lambda e: e.dma_start(out=d_dbgQ, in_=BQT[:]),
                    deps=[fl], inc=dmaX, dma=True)
            em.emit("scalar", lambda e: e.dma_start(out=d_dbgK, in_=KT[:]),
                    deps=[fl], inc=dmaX, dma=True)
            em.emit("scalar", lambda e: e.dma_start(out=d_dbgV, in_=V[:]),
                    deps=[fl], inc=dmaX, dma=True)
            em.emit("scalar", lambda e: e.dma_start(out=d_dbg1, in_=out1[:]),
                    deps=[fl], inc=dmaX, dma=True)
            em.emit("scalar", lambda e: e.dma_start(out=d_dbgS, in_=stat[:]),
                    deps=[fl], inc=dmaX, dma=True)
            return "stop"
        st_done["WS"] = em.dma_total(dmaG)
        wo_ld = wload(b, w_chunks(d_wo, WS, DT, D), [last_q_group, st_done.get("WS")], sc_w["wo"],
                      WS[:].rearrange("p a b -> p (a b)"), sc_slices=sc_slices_w(sc_w["wo"], DT, D))

        pend_attn = None
        attn_done = [None] * NT
        ctx_read_ppbuf = []
        ctx_sb_readers = []
        ctx_evac = []
        prev_exp_par = [None] * 4
        rs_ms = [None] * NT
        ln2_done = [None] * NT
        ln2_sq_last = []
        prev_ppbuf_rd = []

        def pre_ln2(t):
            stats = ln_stats(t, out1[:, t, :], [attn_done[t]], ln_prev,
                             lnstats[:, 2 * t:2 * t + 1], lnstats[:, 2 * t + 1:2 * t + 2],
                             cv_f[:, t % 2, :], src_sdeps=[attn_done[t]])
            ln2_done[t] = stats
            ln2_sq_last.append(stats[0])
            del ln2_sq_last[:-1]

        def emit_attn_groups(i):
            base = (i % 4) * 16
            bw, bdep = banks.acquire(2, which=[4, 5])
            gs = []
            for n in range(2):
                g = group((ctx_evac + [wo_ld] + bdep) if n == 0 else [],
                          banks.f32(4 + n),
                          [(ctxT_sb[:, a * 128:(a + 1) * 128], WS[:, a, n * 512:(n + 1) * 512])
                           for a in range(8)])
                gs.append(g)

            def emit_stts():
                sts = []
                for n in range(2):
                    st = em.emit("vector",
                                 lambda e, i=i, n=n, base=base: e.scalar_tensor_tensor(
                                     out=out1[:, i, n * 512:(n + 1) * 512],
                                     in0=banks.f32(4 + n), scalar=S_(base, 15),
                                     in1=out1[:, i, n * 512:(n + 1) * 512],
                                     op0=ALU.mult, op1=ALU.add),
                                 deps=[gs[n]], sdeps=[rs_ms[i]], inc=vE)
                    sts.append(st)
                banks.release([4, 5], sts)
                attn_done[i] = sts
            return gs, emit_stts

        s_last_ms = None

        def emit_scores(i):
            nonlocal s_last_ms
            sg = []
            for j in range(i // 4 + 1):
                wj = min(512, (i + 1) * 128 - j * 512)
                bw, bdep = banks.acquire(1, which=[j])
                g = group(bdep + [qT_ready[i // 4], kT_ready[j]],
                          pall[:, j * 512: j * 512 + wj],
                          [(BQT[:, a, i * 128:(i + 1) * 128], KT[:, a, j * 512:j * 512 + wj])
                           for a in range(8)])
                sg.append(g)
            s_last_ms = sg[-1]
            return sg

        for i in range(NT):
            base = (i % 4) * 16
            nch = i // 4 + 1
            s_groups = emit_scores(i)
            attn_stt = None
            if pend_attn is not None:
                ctx_sb_readers, attn_stt = pend_attn()
                pend_attn = None
            mx = []
            for j in range(nch):
                wj = min(512, (i + 1) * 128 - j * 512)
                deps = [s_groups[j]]
                mx.append(em.emit("vector",
                                  lambda e, j=j, wj=wj, base=base: e.reduce_max(
                                      out=S_(base, 8 + j), in_=pall[:, j * 512: j * 512 + wj], axis=AX.X),
                                  deps=deps, inc=vE))
            nmneg = em.emit("vector", lambda e, nch=nch, base=base: e.reduce_max(out=S_(base, 12),
                                                                                in_=S_(base, 8, nch),
                                                                                axis=AX.X, negate=True),
                            deps=[prev_exp_par[i % 4]], sdeps=mx, inc=vE)
            nm32 = em.emit("vector", lambda e, base=base: e.tensor_scalar(out=S_(base, 13), in0=S_(base, 12),
                                                                          scalar1=1.0 / 32, scalar2=None,
                                                                          op0=ALU.mult), sdeps=[nmneg], inc=vE)
            if attn_stt is not None:
                attn_stt()
            exp_ms = []
            pgates = []
            for j in range(nch):
                wj = min(512, (i + 1) * 128 - j * 512)
                ex = em.emit("scalar",
                             lambda e, j=j, wj=wj, base=base: e.activation(
                                 out=ppbuf[:, j * 512: j * 512 + wj],
                                 in_=pall[:, j * 512: j * 512 + wj], func=AF.Exp,
                                 scale=1.0 / 32, bias=S_(base, 13)),
                             deps=[nm32] + ctx_read_ppbuf + prev_ppbuf_rd, inc=sE)
                exp_ms.append(ex)
                banks.release([j], [ex])
                if j == nch - 1:
                    zero_ms = em.emit("vector",
                                      lambda e, i=i: e.tensor_tensor(
                                          out=ppbuf[:, i * 128:(i + 1) * 128],
                                          in0=ppbuf[:, i * 128:(i + 1) * 128],
                                          in1=tri01[:], op=ALU.mult),
                                      deps=[ex], inc=vE)
                    pgates.append([zero_ms])
                else:
                    pgates.append([ex])
            psum_ms = []
            pt_ev = []
            for j in range(nch):
                wj = min(512, (i + 1) * 128 - j * 512)
                ex = exp_ms[j]
                pgate = pgates[j]
                psum_ms.append(em.emit("vector",
                                       lambda e, j=j, wj=wj, base=base: e.reduce_sum(
                                           out=S_(base, 8 + j),
                                           in_=ppbuf[:, j * 512: j * 512 + wj], axis=AX.X),
                                       deps=[ex], sdeps=pgate, inc=vE))
                nkc = wj // 128
                bnk, bdep = banks.acquire(1, which=[6 + (j % 2)])
                trs = None
                for k in range(nkc):
                    trs = em.emit("tensor",
                                  lambda e, j=j, k=k, b_=bnk[0]: e.transpose(
                                      banks.bf16(b_)[:, k * 128:(k + 1) * 128],
                                      ppbuf[:, j * 512 + k * 128: j * 512 + (k + 1) * 128],
                                      ident_bf[:]),
                                  deps=(pgate + bdep) if k == 0 else (),
                                  inc=tE if k == nkc - 1 else None)
                ev = em.emit("vector",
                             lambda e, j=j, wj=wj, b_=bnk[0]: e.tensor_copy(
                                 out=ppbuf[:, j * 512: j * 512 + wj],
                                 in_=banks.bf16(b_)[:, 0:wj]),
                             deps=[trs], sdeps=[psum_ms[-1]], inc=vE)
                banks.release(bnk, [ev])
                pt_ev.append(ev)
            prev_exp_par[i % 4] = exp_ms[-1]
            prev_ppbuf_rd = list(psum_ms) + pt_ev
            rsum = em.emit("vector", lambda e, nch=nch, base=base: e.reduce_sum(out=S_(base, 14),
                                                                               in_=S_(base, 8, nch),
                                                                               axis=AX.X),
                           sdeps=psum_ms, inc=vE)
            rs_ms[i] = em.emit("vector", lambda e, base=base: e.reciprocal(out=S_(base, 15), in_=S_(base, 14)),
                               sdeps=[rsum], inc=vE)
            bw, bdep = banks.acquire(2, which=[4, 5])
            ctx_g = []
            for a in range(8):
                g = group((pt_ev + bdep + [v_ready[i]]) if a == 0 else [],
                          pall[:, 2048 + a * 128: 2048 + (a + 1) * 128],
                          [(V[:, kc, a * 128:(a + 1) * 128], ppbuf[:, kc * 128:(kc + 1) * 128])
                           for kc in range(i + 1)])
                ctx_g.append(g)
            ctx_read_ppbuf = [ctx_g[-1]]
            ctx_last_ms = ctx_g[-1]
            ev = em.emit("vector",
                         lambda e: e.tensor_copy(out=ctxT_sb[:], in_=pall[:, 2048:3072]),
                         deps=[ctx_g[-1]] + ctx_sb_readers, inc=vE)
            ctx_evac = [ev]
            banks.release([4, 5], [ev])
            pend_attn = (lambda i=i: emit_attn_groups(i))
            if i >= 1:
                pre_ln2(i - 1)
        if pend_attn is not None:
            gs, sttfn = pend_attn()
            sttfn()
            pend_attn = None
        pre_ln2(NT - 1)
        stg_free[0] = stg_free[0] + ln2_sq_last
        stg_free[1] = stg_free[1] + ln2_sq_last
        ws_free = [g for g in (attn_done[NT - 1] or [])]
        if _DBG and b == 0:
            fl = dbg_flush([m for ms in attn_done for m in (ms or [])] + [rs_ms[14], rs_ms[15]])
            em.emit("scalar", lambda e: e.dma_start(out=d_dbg1, in_=out1[:]),
                    deps=[fl], inc=dmaX, dma=True)
            em.emit("scalar", lambda e: e.dma_start(out=d_dbgS, in_=stat[:]),
                    deps=[fl], inc=dmaX, dma=True)
        if _STOP == "p2" and b == 0:
            return "stop"

        add1_par = [None, None]
        st_done["EW2"] = em.dma_total(dmaG)
        rw1_ld = wload(b, w_chunks(d_rw1, EW2, DT, D), [ctx_last_ms, st_done.get("EW2")], sc_w["rw1"],
                      EW2[:, 0:8, :].rearrange("p a b -> p (a b)"), sc_slices=sc_slices_w(sc_w["rw1"], DT, D))
        ln2_ready = [None] * NT
        hT_ready = [None] * 4
        w_ready = [None] * NT
        eb2_added = [None] * NT
        last_h_group = None
        last_logit_group = None
        wT_readers = []
        pend_tails = []

        def make_tail(t):
            def tail():
                nonlocal wT_readers
                wb = em.emit("vector", lambda e, t=t: e.tensor_copy(
                    out=wbf_sb[:, 0:8], in_=w_sb[:, t * 8:(t + 1) * 8]),
                             deps=wT_readers, sdeps=[w_ready[t]], inc=vE)
                bnk2, bd2 = banks.acquire(1)
                tr = em.emit("tensor",
                             lambda e, b_=bnk2[0]: e.transpose(banks.bf16(b_)[0:9, 0:128],
                                                               wbf_sb[:], ident_bf[:]),
                             deps=[wb] + bd2, inc=tE)
                wtc = em.emit("vector",
                              lambda e, t=t, b_=bnk2[0]: e.tensor_copy(
                                  out=wT_sb[:, t * 128:(t + 1) * 128],
                                  in_=banks.bf16(b_)[0:9, 0:128]),
                              deps=[tr], inc=vE)
                banks.release(bnk2, [wtc])
                bl2, bdd = banks.acquire(2)
                adds = []
                for n in range(2):
                    ops2 = [(wT_sb[:, t * 128:(t + 1) * 128], ebplus[:, n * 512:(n + 1) * 512])]
                    g2 = group([wtc] + (bdd if n == 0 else []), banks.f32(bl2[n]), ops2)
                    ad = em.emit("vector",
                                 lambda e, t=t, n=n, b_=bl2[n]: e.tensor_tensor(
                                     out=out1[:, t, n * 512:(n + 1) * 512],
                                     in0=out1[:, t, n * 512:(n + 1) * 512],
                                     in1=banks.f32(b_), op=ALU.add),
                                 deps=[g2], inc=vE)
                    adds.append(ad)
                    banks.release([bl2[n]], [ad])
                wT_readers = [tr]
                eb2_added[t] = adds
            return tail

        for c in range(4):
            for t in range(4 * c, 4 * c + 4):
                dd = {"lnbf_war": lnbf_war, "dstT": LN2T, "dst_war": [s_last_ms]}
                ev, norm, trs = ln_finish(t, out1[:, t, :],
                                          lnstats[:, 2 * t:2 * t + 1],
                                          lnstats[:, 2 * t + 1:2 * t + 2],
                                          ln2_done[t], dd)
                ln2_ready[t] = ev
                add1 = em.emit("vector",
                               lambda e, t=t: e.tensor_tensor(out=out1[:, t, :], in0=out1[:, t, :],
                                                              in1=ln_bf[:, t % 2, :], op=ALU.add),
                               deps=[norm], inc=vE)
                lnbf_war[t % 2] = [trs, add1]
                add1_par[t % 2] = add1
            cdep = [ln2_ready[4 * c + i] for i in range(4)]
            for a in range(8):
                bnk, bd = banks.acquire(1)
                g = group(cdep + bd + [rw1_ld],
                          banks.f32(bnk[0]),
                          [(EW2[:, d, a * 128:(a + 1) * 128], LN2T[:, d, c * 512:(c + 1) * 512])
                           for d in range(DT)])
                ev = em.emit("scalar",
                             lambda e, a=a, c=c, b_=bnk[0]: e.activation(
                                 out=CH[:, a, c * 512:(c + 1) * 512], in_=banks.f32(b_),
                                 func=AF.Silu, bias=rb1col[:, a:a + 1]),
                             deps=[g, s_last_ms], inc=sE)
                banks.release(bnk, [ev])
                last_h_group = g
                hT_ready[c] = ev
            for tl in pend_tails:
                tl()
            pend_tails = []
            for t in range(4 * c, 4 * c + 4):
                bnk, bd = banks.acquire(1)
                pl = banks.f32(bnk[0])[:, 0:8]
                ops = [(CH[:, a, t * 128:(t + 1) * 128], rw2sb[:, a, :]) for a in range(8)]
                ops.append((ones_bf[0:1, :], rb2row[0:1, :]))
                g = group([hT_ready[c], rw2c] + bd + bias_dep, pl, ops)
                last_logit_group = g
                m1 = em.emit("vector", lambda e, pl=pl: e.reduce_max(out=S_(0, 8), in_=pl, axis=AX.X),
                             deps=[g], inc=vE)
                ngm = em.emit("vector", lambda e: e.tensor_scalar(out=S_(0, 9), in0=S_(0, 8), scalar1=-1.0,
                                                                  scalar2=None, op0=ALU.mult),
                              sdeps=[m1], inc=vE)
                eq = em.emit("vector", lambda e, pl=pl: e.tensor_scalar(out=rsc[:, 0, :], in0=pl,
                                                                        scalar1=S_(0, 8), scalar2=None,
                                                                        op0=ALU.is_equal),
                             sdeps=[m1], inc=vE)
                l2 = em.emit("vector", lambda e, pl=pl: e.scalar_tensor_tensor(out=rsc[:, 1, :], in0=rsc[:, 0, :],
                                                                               scalar=-1e9, in1=pl,
                                                                               op0=ALU.mult, op1=ALU.add),
                             sdeps=[eq], inc=vE)
                m2 = em.emit("vector", lambda e: e.reduce_max(out=S_(0, 10), in_=rsc[:, 1, :], axis=AX.X),
                             sdeps=[l2], inc=vE)
                ge = em.emit("vector", lambda e, pl=pl: e.tensor_scalar(out=rsc[:, 2, :], in0=pl,
                                                                        scalar1=S_(0, 10), scalar2=None,
                                                                        op0=ALU.is_ge),
                             sdeps=[m2], inc=vE)
                exr = rsc[:, 4 + (t % 2), :]
                ex = em.emit("scalar",
                             lambda e, pl=pl, exr=exr: e.activation(out=exr, in_=pl, func=AF.Exp,
                                                                    bias=S_(0, 9)),
                             deps=[ngm] + ([] if t < 2 else [w_ready[t - 2]]), inc=sE)
                banks.release(bnk, [ex, ge])
                me = em.emit("vector", lambda e, exr=exr: e.tensor_tensor(out=rsc[:, 3, :], in0=exr,
                                                                          in1=rsc[:, 2, :], op=ALU.mult),
                             deps=[ex], sdeps=[ge], inc=vE)
                rsm = em.emit("vector", lambda e: e.reduce_sum(out=S_(0, 11), in_=rsc[:, 3, :], axis=AX.X),
                              sdeps=[me], inc=vE)
                rcp = em.emit("vector", lambda e: e.reciprocal(out=S_(0, 12), in_=S_(0, 11)),
                              sdeps=[rsm], inc=vE)
                wt = em.emit("vector", lambda e, t=t: e.tensor_scalar(out=w_sb[:, t * 8:(t + 1) * 8],
                                                                      in0=rsc[:, 3, :], scalar1=S_(0, 12),
                                                                      scalar2=None, op0=ALU.mult),
                             sdeps=[rcp], inc=vE)
                w_ready[t] = wt
                pend_tails.append(make_tail(t))
        ch_free = [last_logit_group]
        ew2_free = [last_h_group]
        if _DBG and b == 0:
            fl = dbg_flush([m for m in w_ready] + ln2_ready + [m for m in eb2_added if m] + [nm for nm, _ in ln2_done])
            em.emit("scalar", lambda e: e.dma_start(out=d_dbg2, in_=w_sb[:]),
                    deps=[fl], inc=dmaX, dma=True)
            em.emit("scalar", lambda e: e.dma_start(out=d_dbgK, in_=LN2T[:]),
                    deps=[fl], inc=dmaX, dma=True)
            em.emit("scalar", lambda e: e.dma_start(out=d_dbgS[:, 0:32], in_=lnstats[:]),
                    deps=[fl], inc=dmaX, dma=True)
            em.emit("scalar", lambda e: e.dma_start(out=d_dbg1, in_=out1[:]),
                    deps=[fl], inc=dmaX, dma=True)
        if _STOP == "p3" and b == 0:
            return "stop"

        store_q = []
        if b == 0:
            stg_free[2] = [ctx_last_ms]
            stg_free[3] = [ln2_ready[NT - 1]] + lnbf_war[0] + lnbf_war[1]
        for tl in pend_tails:
            tl()
        pend_tails = []
        ch_chunks = w_chunks(d_ew1[0], CH, DT, H)
        ch_sc = sc_slices_w(sc_e1[0], DT, H)
        if b == 0:
            h0 = stage_cast([ch_chunks[i] for i in range(0, 16, 2)],
                            ch_free + [st_done.get("CH")], eng="scalar",
                            scratch=[ch_sc[i] for i in range(0, 16, 2)], ring=4)
            h1 = stage_cast([ch_chunks[i] for i in range(1, 16, 2)], [], eng="scalar",
                            scratch=[ch_sc[i] for i in range(1, 16, 2)], ring=4)
            ew1_pair = (h0, h1)
        else:
            ms0 = em.emit("sync", lambda e: e.dma_start(
                out=CH[:].rearrange("p a b -> p (a b)"), in_=sc_e1[0]),
                          deps=ch_free + [st_done.get("CH")], inc=dmaW, dma=True)
            ew1_pair = (ms0, ms0)
        st_done["CH"] = em.dma_total(dmaG)
        st_done["EW2"] = em.dma_total(dmaG)
        ew2_ms = wload(b, w_chunks(d_ew2[0], EW2, HT, D), ew2_free + [st_done.get("EW2")], sc_e2[0],
                       EW2[:].rearrange("p a b -> p (a b)"),
                       sc_slices=sc_slices_w(sc_e2[0], HT, D), ring=4)
        for e_ in range(NE):
            last_eh_group = None
            last_eo_group = None
            for c in range(4):
                silu_last = None
                for m in range(HT):
                    bnk, bd = banks.acquire(1)
                    g = group(bd + [ew1_pair[0] if m < 8 else ew1_pair[1]],
                              banks.f32(bnk[0]),
                              [(CH[:, d, m * 128:(m + 1) * 128], LN2T[:, d, c * 512:(c + 1) * 512])
                               for d in range(DT)])
                    ev = em.emit("scalar",
                                 lambda e, ee=e_, m=m, b_=bnk[0]: e.activation(
                                     out=EHT[:, m, :], in_=banks.f32(b_), func=AF.Silu,
                                     bias=eb1cols[:, ee * 16 + m: ee * 16 + m + 1]),
                                 deps=[g] + eht_war, inc=sE)
                    banks.release(bnk, [ev])
                    silu_last = ev
                    last_eh_group = g
                silu_flush = em.emit("scalar",
                                     lambda e: e.activation(out=stat[:, 67:68], in_=stat[:, 67:68],
                                                            func=AF.Identity),
                                     sdeps=[silu_last, flush_prev["scalar"]], inc=sE)
                flush_prev["scalar"] = silu_flush
                if c == 3 and e_ < NE - 1:
                    st_done["CH"] = em.dma_total(dmaG)
                    next_ew1_ms = wload(b, w_chunks(d_ew1[e_ + 1], CH, DT, H),
                                        [last_eh_group, st_done.get("CH")], sc_e1[e_ + 1],
                                        CH[:].rearrange("p a b -> p (a b)"), eng="scalar",
                                        sc_slices=sc_slices_w(sc_e1[e_ + 1], DT, H), ring=4)
                eo_groups = []
                for s4 in range(4):
                    t = 4 * c + s4
                    bl, bd = banks.acquire(2)
                    evs = []
                    for n in range(2):
                        g = group(([silu_flush, ew2_ms] + bd) if n == 0 else [silu_flush],
                                  banks.f32(bl[n]),
                                  [(EHT[:, m, s4 * 128:(s4 + 1) * 128], EW2[:, m, n * 512:(n + 1) * 512])
                                   for m in range(HT)])
                        eo_groups.append(g)
                        st = em.emit("vector",
                                     lambda e, t=t, n=n, ee=e_, b_=bl[n]: e.scalar_tensor_tensor(
                                         out=out1[:, t, n * 512:(n + 1) * 512],
                                         in0=banks.f32(b_),
                                         scalar=w_sb[:, t * 8 + ee: t * 8 + ee + 1],
                                         in1=out1[:, t, n * 512:(n + 1) * 512],
                                         op0=ALU.mult, op1=ALU.add),
                                     deps=[g], sdeps=[w_ready[t]] + (eb2_added[t] or []), inc=vE)
                        evs.append(st)
                        banks.release([bl[n]], [st])
                    last_eo_group = eo_groups[-1]
                    if e_ == NE - 1:
                        store_q.append((t, list(evs)))
                eht_war = eo_groups[-2:]
                if e_ == NE - 1:
                    cevs = [ms for _, evs2 in store_q[-4:] for ms in evs2]
                    flush = em.emit("vector", lambda e: e.memset(stat[:, 65:66], 0.0),
                                    sdeps=cevs + [flush_prev["vector"]], inc=vE)
                    flush_prev["vector"] = flush
                    dst = d_out[b].rearrange("(t p) d -> p t d", p=128)[:, 4 * c:4 * c + 4, :]
                    om = em.emit("scalar",
                                 lambda e, dst=dst, c=c: e.dma_start(
                                     out=dst, in_=out1[:, 4 * c:4 * c + 4, :]),
                                 deps=[flush] + cevs, inc=dmaX, dma=True)
                    out_dma_ms.append(om)
            if _DBG and b == 0 and e_ in (0, 3):
                dst = d_dbg3 if e_ == 0 else d_dbg4
                fl = dbg_flush([evs])
                em.emit("scalar", lambda e, dst=dst: e.dma_start(out=dst, in_=out1[:]),
                        deps=[fl], inc=dmaX, dma=True)
            if e_ < NE - 1:
                st_done["EW2"] = em.dma_total(dmaG)
                ew1_pair = (next_ew1_ms, next_ew1_ms)
                ew2_ms = wload(b, w_chunks(d_ew2[e_ + 1], EW2, HT, D),
                               [last_eo_group, st_done.get("EW2")],
                               sc_e2[e_ + 1], EW2[:].rearrange("p a b -> p (a b)"),
                               sc_slices=sc_slices_w(sc_e2[e_ + 1], HT, D), ring=4)
            ch_free = [last_eh_group]
            ew2_free = [last_eo_group]
        return None

    r = emit_batch(0)
    if r != "stop":
        em.barrier(resources=(banks,))
        emit_batch(1)
    em.barrier(resources=(banks,))

    with nc.Block() as blk:
        em.finish(blk)
    return nc


_CACHED = None


def _get_program():
    global _CACHED
    if _CACHED is None:
        _CACHED = build_program()
    return _CACHED


def kernel(L0_attn_y_all, L0_attn_Wq, L0_attn_bq, L0_attn_Wk, L0_attn_bk,
           L0_attn_Wv, L0_attn_bv, L0_attn_Wo, L0_attn_bo,
           L0_ffn_rw1, L0_ffn_rb1, L0_ffn_rw2, L0_ffn_rb2,
           L0_ffn_ew1, L0_ffn_eb1, L0_ffn_ew2, L0_ffn_eb2):
    nc = _get_program()
    f = np.float32
    x = np.ascontiguousarray(np.asarray(L0_attn_y_all, dtype=f))
    bq = np.asarray(L0_attn_bq, f).reshape(8, 128)
    bk = np.asarray(L0_attn_bk, f).reshape(8, 128)
    rb1 = np.asarray(L0_ffn_rb1, f).reshape(8, 128)
    eb1 = np.asarray(L0_ffn_eb1, f).reshape(NE, 16, 128)
    rw2 = np.asarray(L0_ffn_rw2, f).reshape(8, 128, 8)
    biaspack = np.zeros((128, 1280), dtype=f)
    biaspack[:, 0:1024] = np.asarray(L0_attn_bv, f).reshape(1, D)
    biaspack[:, 1024:1032] = bq.T
    biaspack[:, 1032:1040] = bk.T
    biaspack[:, 1040:1048] = rb1.T
    biaspack[:, 1048:1176] = eb1.transpose(2, 0, 1).reshape(128, 128)
    biaspack[:, 1176:1240] = rw2.transpose(1, 0, 2).reshape(128, 64)
    rowpack = np.zeros((128, 1032), dtype=f)
    rowpack[0:8, 0:1024] = np.asarray(L0_ffn_eb2, f)
    rowpack[8, 0:1024] = np.asarray(L0_attn_bo, f)
    rowpack[0, 1024:1032] = np.asarray(L0_ffn_rb2, f)
    base = {
        "wq": np.ascontiguousarray(np.asarray(L0_attn_Wq, f)),
        "wk": np.ascontiguousarray(np.asarray(L0_attn_Wk, f)),
        "wv": np.ascontiguousarray(np.asarray(L0_attn_Wv, f)),
        "wo": np.ascontiguousarray(np.asarray(L0_attn_Wo, f)),
        "rw1": np.ascontiguousarray(np.asarray(L0_ffn_rw1, f)),
        "ew1": np.ascontiguousarray(np.asarray(L0_ffn_ew1, f)),
        "ew2": np.ascontiguousarray(np.asarray(L0_ffn_ew2, f)),
        "idbf": np.eye(128, dtype=ml_dtypes.bfloat16),
        "tri01": np.tril(np.ones((128, 128), dtype=ml_dtypes.bfloat16)),
        "biaspack": biaspack,
        "rowpack": rowpack,
    }
    in_maps = []
    for c in range(NCORES):
        m = dict(base)
        m["x"] = np.ascontiguousarray(x[2 * c:2 * c + 2])
        in_maps.append(m)
    res = run_bass_kernel_spmd(nc, in_maps, core_ids=list(range(NCORES)))
    out = np.concatenate([res.results[c]["out"] for c in range(NCORES)], axis=0)
    if _DBG:
        kernel._dbg = [res.results[c] for c in range(NCORES)]
    return out.reshape(16, T, D)
